# version 06 (final)
# baseline (speedup 1.0000x reference)
import numpy as np
import ml_dtypes
from contextlib import ExitStack
import concourse.bass as bass
import concourse.mybir as mybir
from concourse.bass_utils import run_bass_kernel_spmd
from concourse.bass_core_rust import IndirectOffsetOnAxis as IOA

F32 = mybir.dt.float32
BF16 = mybir.dt.bfloat16
AF = mybir.ActivationFunctionType
ALU = mybir.AluOpType
AX = mybir.AxisListType
BF = ml_dtypes.bfloat16

NCORE = 8
BSZ, SEQ, D = 4, 8192, 1024
TT = BSZ * SEQ
TPC = TT // NCORE
TA = 4224
WC = 16384

RG_PAIR = [[0, 1], [2, 3], [4, 5], [6, 7]]
FIRK = [1, 3, 7, 15, 31]
FIROFF = [0, 1, 4, 11, 26]
RG_ALL = [[0, 1, 2, 3, 4, 5, 6, 7]]


class P2:

    def __init__(self, nc, block, stack):
        self.nc, self.block, self.stack = nc, block, stack
        self.prev = []
        self._pend = None
        self._mk()

    def _mk(self):
        self.sm = self.stack.enter_context(self.nc.semaphore())
        self.n = 0
        self.seen = set()

    def stage(self):
        self.flush()
        self.prev.append((self.sm, self.n))
        self._mk()

    def flush(self):
        if self._pend is not None:
            engname, funcs = self._pend
            self._pend = None
            getattr(self.block, engname)(lambda e: [f(e) for f in funcs])

    def on(self, engname, f):
        if self.n > 52000:
            self.stage()
        if self._pend is not None and self._pend[0] == engname:
            self._pend[1].append(f)
        else:
            self.flush()
            self._pend = (engname, [f])

    def _bar(self, eng):
        k = id(eng)
        if k not in self.seen:
            self.seen.add(k)
            for (sem, v) in self.prev:
                if v:
                    eng.wait_ge(sem, v)

    def t(self, instr):
        instr.then_inc(self.sm, 1)
        self.n += 1

    def dma(self, q, out, in_):
        self._bar(q)
        q.dma_start(out=out, in_=in_).then_inc(self.sm, 16)
        self.n += 16

    def sto(self, q, out, in_):
        self._bar(q)
        q.dma_start(out=out, in_=in_).then_inc(self.sm, 16)
        self.n += 16

    @property
    def ld(self):
        return self.sm

    @property
    def l(self):
        return self.n

    @l.setter
    def l(self, v):
        self.n = v

    @property
    def st(self):
        return self.sm

    @property
    def s(self):
        return self.n

    def gate(self, eng):
        self._bar(eng)
        eng.wait_ge(self.sm, self.n)

    def gatec(self, eng):
        self._bar(eng)
        eng.wait_ge(self.sm, self.n)

    def gates(self, eng):
        self._bar(eng)
        eng.wait_ge(self.sm, self.n)


def ln_tile(pr, x_in, out_bf, sc):
    v, s = 'vector', 'scalar'
    cols = sc['cols'][:]
    sq = sc['sq'][:]

    def f1(e):
        pr.gate(e)
        pr.t(e.reduce_sum(cols[:, 0:1], x_in, AX.X))
    pr.on(v, f1)

    def f2(e):
        pr.gatec(e)
        pr.t(e.activation(sq, x_in, AF.Square))
    pr.on(s, f2)

    def f3(e):
        pr.gatec(e)
        pr.t(e.reduce_sum(cols[:, 1:2], sq, AX.X))
        pr.gatec(e)
        pr.t(e.tensor_scalar_mul(cols[:, 0:1], cols[:, 0:1], 1.0 / D))
        pr.gatec(e)
        pr.t(e.tensor_scalar_mul(cols[:, 1:2], cols[:, 1:2], 1.0 / D))
        pr.gatec(e)
        pr.t(e.tensor_mul(cols[:, 2:3], cols[:, 0:1], cols[:, 0:1]))
        pr.gatec(e)
        pr.t(e.tensor_sub(cols[:, 3:4], cols[:, 1:2], cols[:, 2:3]))
    pr.on(v, f3)

    def f4(e):
        pr.gatec(e)
        pr.t(e.activation(cols[:, 4:5], cols[:, 3:4], AF.Ln, bias=sc['eps']))
        pr.gatec(e)
        pr.t(e.activation(cols[:, 4:5], cols[:, 4:5], AF.Exp, scale=-0.5))
    pr.on(s, f4)

    def f5(e):
        pr.gatec(e)
        pr.t(e.scalar_tensor_tensor(cols[:, 5:6], cols[:, 0:1], -1.0, cols[:, 4:5],
                                    ALU.mult, ALU.mult))
        pr.gatec(e)
        pr.t(e.tensor_scalar(out_bf, x_in, cols[:, 4:5], cols[:, 5:6], ALU.mult, ALU.add))
    pr.on(v, f5)


def mkconsts(pr, stack, nc, vals, name):
    kt = stack.enter_context(nc.sbuf_tensor(name, [128, max(1, len(vals))], F32))
    def f(e):
        pr.gatec(e)
        for i, v in enumerate(vals):
            pr.t(e.memset(kt[:, i:i + 1], float(v)))
    pr.on('vector', f)
    return {v: kt[:, i:i + 1] for i, v in enumerate(vals)}


WSPECS = [
    ('l0_wgT', (1024, 8)), ('l0_winT', (1024, 4160)), ('l0_woutT', (2048, 1024)),
    ('l0_wgateT', (8, 1024, 4096)), ('l0_wupT', (8, 1024, 4096)), ('l0_wdownT', (8, 4096, 1024)),
    ('l1_w1T', (1024, 4096)), ('l1_w2T', (1024, 4096)), ('l1_h1T', (4096, 4096)),
    ('l1_h2T', (4096, 4096)), ('l1_w3T', (4096, 1024)),
    ('l1_wq', (1024, 1024)), ('l1_wk', (1024, 1024)), ('l1_wv', (1024, 1024)),
    ('l1_wb', (1024, 4)), ('l1_gw1', (1052, 2048)), ('l1_gw2', (2048, 28)),
    ('l1_wo', (1024, 1024)),
]


WSPLIT = 5120


def wlayout():
    offs = {}
    row = 0
    for name, shp in WSPECS:
        if name == 'l0_wdownT':
            row = WSPLIT
        n = int(np.prod(shp))
        nr = (n + WC - 1) // WC
        offs[name] = (row, shp)
        row += nr
    rtot = ((row + 1023) // 1024) * 1024
    return offs, rtot


WOFFS, WROWS = wlayout()
WPC = WROWS // NCORE


def pack_weights(wdict):
    big = np.zeros((WROWS, WC), np.float32)
    for name, shp in WSPECS:
        row, _ = WOFFS[name]
        flat = np.ascontiguousarray(wdict[name]).reshape(-1)
        nr = (flat.size + WC - 1) // WC
        pad = np.zeros(nr * WC, np.float32)
        pad[:flat.size] = flat
        big[row:row + nr] = pad.reshape(nr, WC)
    return big


def wap(wbf0, wbf1, name):
    row, shp = WOFFS[name]
    n = int(np.prod(shp))
    B = shp[-1]
    nr = (n + WC - 1) // WC
    if row >= WSPLIT:
        src = wbf1[row - WSPLIT: row - WSPLIT + nr, :]
    else:
        src = wbf0[row:row + nr, :]
    return src.rearrange("r c -> (r c)")[0:n].rearrange("(a b) -> a b", b=B)



def emit_wcast(pr, nc, stack, px, win, wsh):
    CH = 4096
    sb = stack.enter_context(nc.sbuf_tensor(px + "sb", [128, 2 * CH], F32))
    ob = stack.enter_context(nc.sbuf_tensor(px + "ob", [128, 2 * CH], BF16))
    for rt in range(WPC // 128):
        for i in range(WC // CH):
            buf = (i % 2) * CH

            def fl(e, rt=rt, i=i, buf=buf):
                pr.gatec(e)
                pr.gates(e)
                pr.dma(e, sb[:, buf:buf + CH], win[rt * 128:(rt + 1) * 128, i * CH:(i + 1) * CH])
            pr.on('sync', fl)

            def fc(e, buf=buf):
                pr.gate(e)
                pr.t(e.copy(ob[:, buf:buf + CH], sb[:, buf:buf + CH]))
            pr.on('scalar', fc)

            def fs(e, rt=rt, i=i, buf=buf):
                pr.gatec(e)
                pr.sto(e, wsh[rt * 128:(rt + 1) * 128, i * CH:(i + 1) * CH], ob[:, buf:buf + CH])
            pr.on('gpsimd', fs)


def emit_wcast2(pr, nc, stack, px, win, wsh0, wsh1, S8):
    CH = 4096
    sb = stack.enter_context(nc.sbuf_tensor(px + "sb", [128, 2 * CH], F32))
    ob = stack.enter_context(nc.sbuf_tensor(px + "ob", [128, 2 * CH], BF16))
    for rt in range(WPC // 128):
        for i in range(WC // CH):
            buf = (i % 2) * CH

            def fl(e, rt=rt, i=i, buf=buf):
                pr.gatec(e)
                pr.gates(e)
                pr.dma(e, sb[:, buf:buf + CH], win[rt * 128:(rt + 1) * 128, i * CH:(i + 1) * CH])
            pr.on('sync', fl)

            def fc(e, buf=buf):
                pr.gate(e)
                pr.t(e.copy(ob[:, buf:buf + CH], sb[:, buf:buf + CH]))
            pr.on('scalar', fc)

            def fs(e, rt=rt, i=i, buf=buf):
                pr.gatec(e)
                if rt * 128 < S8:
                    dst = wsh0[rt * 128:(rt + 1) * 128, i * CH:(i + 1) * CH]
                else:
                    dst = wsh1[rt * 128 - S8:(rt + 1) * 128 - S8, i * CH:(i + 1) * CH]
                pr.sto(e, dst, ob[:, buf:buf + CH])
            pr.on('gpsimd', fs)


def emit_coll(pr, pairs, rg=None):
    if rg is None:
        rg = RG_PAIR

    def f(g):
        pr.gatec(g)
        pr.gates(g)
        for (i_, o_) in pairs:
            pr.t(g.collective_compute("AllGather", ALU.bypass, replica_groups=rg,
                                      ins=[i_.ap().opt()], outs=[o_.ap().opt()]))
        pr.gatec(g)
    pr.on('gpsimd', f)


def emit_moe(pr, nc, stack, px, io):
    xin, out = io['x'], io['out']
    wgt, wgate, wup, wdown = io['wgt'], io['wgate'], io['wup'], io['wdown']
    bg, idb, idf, e8, onesf = io['bg'], io['idb'], io['idf'], io['e8'], io['ones']
    halves = io['halves']
    TH0 = 2048
    sbt = lambda n, shp, dt: stack.enter_context(nc.sbuf_tensor(px + n, shp, dt))
    ps = lambda n, shp: stack.enter_context(nc.psum_tensor(px + n, shp, F32))
    kq = mkconsts(pr, stack, nc, [1e-5], px + "kq")
    xt = sbt("xt", [128, 1024], F32)
    xlt = sbt("xlt", [128, 1024], BF16)
    sq = sbt("sq", [128, 1024], F32)
    cols = sbt("cols", [128, 16], F32)
    sid = sbt("sid", [128, 128], BF16)
    sidf = sbt("sidf", [128, 128], F32)
    sones = sbt("sones", [1, 128], F32)
    se8 = sbt("se8", [8, 8 * 128], F32)
    wgtb = sbt("wgtb", [128, 64], BF16)
    bgb = sbt("bgb", [128, 8], F32)
    bgs = sbt("bgs", [1, 8], F32)
    xf = sbt("xf", [128, 8 * TH0], BF16)
    wtsall = sbt("wtsall", [128, 16 * 8], F32)
    wrow = sbt("wrow", [8, TH0], F32)
    wbc = sbt("wbc", [128, TH0], F32)
    H = sbt("H", [128, 4 * TH0], BF16)
    oacc = sbt("oacc", [128, 8 * TH0], F32)
    wgs = sbt("wgs", [128, 8 * 512], BF16)
    wus = sbt("wus", [128, 8 * 512], BF16)
    wds = sbt("wds", [128, 4 * 1024], BF16)
    tmpG = sbt("tmpG", [128, 512], F32)
    tmpU = sbt("tmpU", [128, 512], F32)
    otok = sbt("otok", [128, 1024], F32)
    pG = ps("pG", [128, 512])
    pU = ps("pU", [128, 512])
    pD = ps("pD", [128, 512])
    pW = ps("pW", [128, 512])
    pL = ps("pL", [128, 8])
    pT = stack.enter_context(nc.psum_tensor(px + "pT", [128, 128], BF16))
    pTf = ps("pTf", [128, 128])

    def f0(e):
        pr.dma(e, sid[:], idb)
        pr.dma(e, sidf[:], idf)
        pr.dma(e, sones[:], onesf)
        pr.dma(e, se8[:], e8)
        pr.dma(e, wgtb[:].rearrange("p (kc j) -> p kc j", j=8),
               wgt.rearrange("(kc p) j -> p kc j", p=128))
        pr.dma(e, bgs[:], bg)
    pr.on('sync', f0)

    def f1b(e):
        pr.gate(e)
        pr.t(e.matmul(pL[:, 0:8], sones[:], bgs[:], start=True, stop=True))
    pr.on('tensor', f1b)

    def f1c(e):
        pr.gatec(e)
        pr.t(e.copy(bgb[:], pL[:, 0:8]))
    pr.on('scalar', f1c)

    for (t0, TH) in halves:
        NR = TH // 128
        TGS = [(g * 512, min(512, TH - g * 512)) for g in range((TH + 511) // 512)]
        for r in range(NR):
            def fl(e, r=r, t0=t0):
                pr.gatec(e)
                pr.dma(e, xt[:], xin[t0 + r * 128: t0 + (r + 1) * 128, :])
            pr.on('sync', fl)
            ln_tile(pr, xt[:], xlt[:], {'sq': sq, 'cols': cols, 'eps': kq[1e-5]})
            for kc2 in range(8):
                def ftr(e, kc2=kc2):
                    pr.gate(e)
                    pr.t(e.transpose(pT[:, 0:128], xlt[:, kc2 * 128:(kc2 + 1) * 128], sid[:]))
                pr.on('tensor', ftr)

                def fcp(e, kc2=kc2, r=r, TH=TH):
                    pr.gatec(e)
                    pr.t(e.copy(xf[:, kc2 * TH + r * 128: kc2 * TH + (r + 1) * 128], pT[:, 0:128]))
                pr.on('scalar', fcp)

            def flg(e, r=r, TH=TH):
                pr.gatec(e)
                for ki in range(8):
                    m = e.matmul(pL[:, 0:8], xf[:, ki * TH + r * 128: ki * TH + (r + 1) * 128],
                                 wgtb[:, ki * 8:(ki + 1) * 8], start=(ki == 0), stop=(ki == 7))
                    if ki == 7:
                        pr.t(m)
            pr.on('tensor', flg)

            def ftp(e, r=r):
                L = wtsall[:, r * 8:(r + 1) * 8]
                pr.gatec(e)
                pr.t(e.tensor_add(L, pL[:, 0:8], bgb[:]))
                pr.gatec(e)
                pr.t(e.reduce_max(cols[:, 8:9], L, AX.X))
                pr.gatec(e)
                pr.t(e.tensor_scalar(sq[:, 0:8], L, cols[:, 8:9], None, ALU.is_ge))
                pr.gatec(e)
                pr.t(e.scalar_tensor_tensor(sq[:, 8:16], sq[:, 0:8], -1e9, L, ALU.mult, ALU.add))
                pr.gatec(e)
                pr.t(e.reduce_max(cols[:, 9:10], sq[:, 8:16], AX.X))
                pr.gatec(e)
                pr.t(e.tensor_sub(cols[:, 10:11], cols[:, 9:10], cols[:, 8:9]))
            pr.on('vector', ftp)

            def ftp2(e):
                pr.gatec(e)
                pr.t(e.activation(cols[:, 11:12], cols[:, 10:11], AF.Exp))
            pr.on('scalar', ftp2)

            def ftp3(e, r=r):
                L = wtsall[:, r * 8:(r + 1) * 8]
                pr.gatec(e)
                pr.t(e.tensor_scalar(sq[:, 16:24], L, cols[:, 9:10], None, ALU.is_ge))
                pr.gatec(e)
                pr.t(e.tensor_scalar(sq[:, 24:32], L, cols[:, 8:9], None, ALU.subtract))
            pr.on('vector', ftp3)

            def ftp4(e):
                pr.gatec(e)
                pr.t(e.activation(sq[:, 32:40], sq[:, 24:32], AF.Exp))
            pr.on('scalar', ftp4)

            def ftp5(e, r=r):
                L = wtsall[:, r * 8:(r + 1) * 8]
                pr.gatec(e)
                pr.t(e.tensor_scalar_add(cols[:, 12:13], cols[:, 11:12], 1.0))
                pr.gatec(e)
                pr.t(e.reciprocal(cols[:, 13:14], cols[:, 12:13]))
                pr.gatec(e)
                pr.t(e.tensor_mul(sq[:, 40:48], sq[:, 16:24], sq[:, 32:40]))
                pr.gatec(e)
                pr.t(e.tensor_scalar(L, sq[:, 40:48], cols[:, 13:14], None, ALU.mult))
            pr.on('vector', ftp5)

            def ftw(e, r=r):
                pr.gate(e)
                pr.t(e.transpose(pTf[0:8, 0:128], wtsall[:, r * 8:(r + 1) * 8], sidf[:]))
            pr.on('tensor', ftw)

            def ftw2(e, r=r):
                pr.gatec(e)
                pr.t(e.copy(wrow[:, r * 128:(r + 1) * 128], pTf[0:8, 0:128]))
            pr.on('scalar', ftw2)

        for ex in range(8):
            for (o2, w2) in TGS:
                def fwb(e, ex=ex, o2=o2, w2=w2):
                    pr.gatec(e)
                    pr.t(e.matmul(pW[:, 0:w2], se8[:, ex * 128:(ex + 1) * 128],
                                  wrow[:, o2:o2 + w2], start=True, stop=True))
                pr.on('tensor', fwb)

                def fwb2(e, o2=o2, w2=w2):
                    pr.gatec(e)
                    pr.t(e.copy(wbc[:, o2:o2 + w2], pW[:, 0:w2]))
                pr.on('scalar', fwb2)
            for s_ in range(8):
                def fws(e, ex=ex, s_=s_):
                    pr.gatec(e)
                    pr.dma(e, wgs[:].rearrange("p (kc j) -> p kc j", j=512),
                           wgate[ex * 1024:(ex + 1) * 1024, s_ * 512:(s_ + 1) * 512]
                           .rearrange("(kc p) j -> p kc j", p=128))
                    pr.dma(e, wus[:].rearrange("p (kc j) -> p kc j", j=512),
                           wup[ex * 1024:(ex + 1) * 1024, s_ * 512:(s_ + 1) * 512]
                           .rearrange("(kc p) j -> p kc j", p=128))
                    pr.dma(e, wds[:].rearrange("p (jc d) -> p jc d", d=1024),
                           wdown[ex * 4096 + s_ * 512: ex * 4096 + (s_ + 1) * 512, :]
                           .rearrange("(jc p) d -> p jc d", p=128))
                pr.on('sync', fws)
                for jo in range(4):
                    for (o2, w2) in TGS:
                        def fmm(e, jo=jo, o2=o2, w2=w2, TH=TH):
                            pr.gate(e)
                            for ki in range(8):
                                m = e.matmul(pG[:, 0:w2], wgs[:, ki * 512 + jo * 128: ki * 512 + (jo + 1) * 128],
                                             xf[:, ki * TH + o2: ki * TH + o2 + w2],
                                             start=(ki == 0), stop=(ki == 7))
                                if ki == 7:
                                    pr.t(m)
                            for ki in range(8):
                                m = e.matmul(pU[:, 0:w2], wus[:, ki * 512 + jo * 128: ki * 512 + (jo + 1) * 128],
                                             xf[:, ki * TH + o2: ki * TH + o2 + w2],
                                             start=(ki == 0), stop=(ki == 7))
                                if ki == 7:
                                    pr.t(m)
                        pr.on('tensor', fmm)

                        def fev(e, w2=w2):
                            pr.gatec(e)
                            pr.t(e.activation(tmpG[:, 0:w2], pG[:, 0:w2], AF.Silu))
                        pr.on('scalar', fev)

                        def fev2(e, jo=jo, o2=o2, w2=w2, TH=TH):
                            pr.gatec(e)
                            pr.t(e.tensor_mul(tmpU[:, 0:w2], pU[:, 0:w2], wbc[:, o2:o2 + w2]))
                            pr.gatec(e)
                            pr.t(e.tensor_mul(H[:, jo * TH + o2: jo * TH + o2 + w2],
                                              tmpG[:, 0:w2], tmpU[:, 0:w2]))
                        pr.on('vector', fev2)
                for do in range(8):
                    for (o2, w2) in TGS:
                        def fmm2(e, do=do, o2=o2, w2=w2, TH=TH):
                            pr.gate(e)
                            for jc in range(4):
                                m = e.matmul(pD[:, 0:w2], wds[:, jc * 1024 + do * 128: jc * 1024 + (do + 1) * 128],
                                             H[:, jc * TH + o2: jc * TH + o2 + w2],
                                             start=(jc == 0), stop=(jc == 3))
                                if jc == 3:
                                    pr.t(m)
                        pr.on('tensor', fmm2)

                        def fac(e, do=do, o2=o2, w2=w2, ex=ex, s_=s_, TH=TH):
                            pr.gatec(e)
                            dst = oacc[:, do * TH + o2: do * TH + o2 + w2]
                            if ex == 0 and s_ == 0:
                                pr.t(e.tensor_copy(dst, pD[:, 0:w2]))
                            else:
                                pr.t(e.tensor_add(dst, dst, pD[:, 0:w2]))
                        pr.on('vector', fac)

        for r in range(NR):
            def flr(e, r=r, t0=t0):
                pr.gatec(e)
                pr.gates(e)
                pr.dma(e, xt[:], xin[t0 + r * 128: t0 + (r + 1) * 128, :])
            pr.on('sync', flr)
            for do in range(8):
                def ftr(e, do=do, r=r, TH=TH):
                    pr.gate(e)
                    pr.t(e.transpose(pTf[:, 0:128], oacc[:, do * TH + r * 128: do * TH + (r + 1) * 128], sidf[:]))
                pr.on('tensor', ftr)

                def fadd(e, do=do):
                    pr.gatec(e)
                    pr.gates(e)
                    pr.t(e.scalar_tensor_tensor(otok[:, do * 128:(do + 1) * 128],
                                                pTf[:, 0:128], 0.5, xt[:, do * 128:(do + 1) * 128],
                                                ALU.mult, ALU.add))
                pr.on('vector', fadd)

            def fst(e, r=r, t0=t0):
                pr.gatec(e)
                pr.sto(e, out[t0 + r * 128: t0 + (r + 1) * 128, :], otok[:])
            pr.on('gpsimd', fst)


def emit_tffn(pr, nc, stack, px, io):
    xin, out = io['x'], io['out']
    w1, w2, h1, h2, w3 = io['w1'], io['w2'], io['h1'], io['h2'], io['w3']
    scl_in = io['scl']
    idb, idf = io['idb'], io['idf']
    chunks = io['chunks']
    sbt = lambda n, shp, dt: stack.enter_context(nc.sbuf_tensor(px + n, shp, dt))
    ps = lambda n, shp: stack.enter_context(nc.psum_tensor(px + n, shp, F32))
    kq = mkconsts(pr, stack, nc, [1e-5], px + "kq")
    TC = 512
    xsb = sbt("xsb", [128, 4 * 1024], F32)
    xlt = sbt("xlt", [128, 1024], BF16)
    sq = sbt("sq", [128, 1024], F32)
    cols = sbt("cols", [128, 8], F32)
    sid = sbt("sid", [128, 128], BF16)
    sidf = sbt("sidf", [128, 128], F32)
    xf = sbt("xf", [128, 8 * TC], BF16)
    Ha = sbt("Ha", [128, 32 * TC], BF16)
    Hb = sbt("Hb", [128, 32 * TC], BF16)
    scl = sbt("scl", [128, 136], F32)
    wsb = sbt("wsb", [128, 2 * 8 * 128], BF16)
    hw = sbt("hw", [128, 32 * 128], BF16)
    w3b = sbt("w3b", [128, 32 * 128], BF16)
    otok = sbt("otok", [128, 1024], F32)
    tmpG = sbt("tmpG", [128, 512], F32)
    pA = ps("pA", [128, 512])
    pB = ps("pB", [128, 512])
    pT = stack.enter_context(nc.psum_tensor(px + "pT", [128, 128], BF16))
    pTf = ps("pTf", [128, 128])

    def f0b(e):
        pr.dma(e, sid[:], idb)
        pr.dma(e, sidf[:], idf)
        pr.dma(e, scl[:], scl_in)
    pr.on('sync', f0b)

    for (t0c, TCW) in chunks:
        NR = TCW // 128
        for r in range(NR):
            def fl(e, t0c=t0c, r=r):
                pr.gatec(e)
                pr.dma(e, xsb[:, r * 1024:(r + 1) * 1024],
                       xin[t0c + r * 128: t0c + (r + 1) * 128, :])
            pr.on('sync', fl)
            ln_tile(pr, xsb[:, r * 1024:(r + 1) * 1024], xlt[:], {'sq': sq, 'cols': cols, 'eps': kq[1e-5]})
            for kc in range(8):
                def ftr(e, kc=kc):
                    pr.gate(e)
                    pr.t(e.transpose(pT[:, 0:128], xlt[:, kc * 128:(kc + 1) * 128], sid[:]))
                pr.on('tensor', ftr)

                def fcp(e, kc=kc, r=r):
                    pr.gatec(e)
                    pr.t(e.copy(xf[:, kc * TC + r * 128: kc * TC + (r + 1) * 128], pT[:, 0:128]))
                pr.on('scalar', fcp)

        for jo in range(32):
            def fw(e, jo=jo):
                pr.gatec(e)
                pr.dma(e, wsb[:, 0:1024].rearrange("p (kc j) -> p kc j", j=128),
                       w1[:, jo * 128:(jo + 1) * 128].rearrange("(kc p) j -> p kc j", p=128))
                pr.dma(e, wsb[:, 1024:2048].rearrange("p (kc j) -> p kc j", j=128),
                       w2[:, jo * 128:(jo + 1) * 128].rearrange("(kc p) j -> p kc j", p=128))
            pr.on('sync', fw)

            def fmm(e, jo=jo, TCW=TCW):
                pr.gate(e)
                for ki in range(8):
                    m = e.matmul(pA[:, 0:TCW], wsb[:, ki * 128:(ki + 1) * 128],
                                 xf[:, ki * TC: ki * TC + TCW], start=(ki == 0), stop=(ki == 7))
                    if ki == 7:
                        pr.t(m)
                for ki in range(8):
                    m = e.matmul(pB[:, 0:TCW], wsb[:, 1024 + ki * 128: 1024 + (ki + 1) * 128],
                                 xf[:, ki * TC: ki * TC + TCW], start=(ki == 0), stop=(ki == 7))
                    if ki == 7:
                        pr.t(m)
            pr.on('tensor', fmm)

            def fev(e, jo=jo, TCW=TCW):
                pr.gatec(e)
                pr.t(e.activation(tmpG[:, 0:TCW], pA[:, 0:TCW], AF.Silu, scale=scl[:, jo:jo + 1]))
            pr.on('scalar', fev)

            def fev2(e, jo=jo, TCW=TCW):
                pr.gatec(e)
                pr.t(e.scalar_tensor_tensor(Ha[:, jo * TC: jo * TC + TCW],
                                            pB[:, 0:TCW], scl[:, 32 + jo:33 + jo], tmpG[:, 0:TCW],
                                            ALU.mult, ALU.mult))
            pr.on('vector', fev2)

        for (hsrc, hdst, wT, scoff) in ((Ha, Hb, h1, 64), (Hb, Ha, h2, 96)):
            for jo in range(32):
                def fw(e, jo=jo, wT=wT):
                    pr.gatec(e)
                    for q in range(4):
                        pr.dma(e, hw[:, q * 1024:(q + 1) * 1024].rearrange("p (kc j) -> p kc j", j=128),
                               wT[q * 1024:(q + 1) * 1024, jo * 128:(jo + 1) * 128]
                               .rearrange("(kc p) j -> p kc j", p=128))
                pr.on('sync', fw)

                def fmm(e, jo=jo, hsrc=hsrc, TCW=TCW):
                    pr.gate(e)
                    for ki in range(32):
                        m = e.matmul(pA[:, 0:TCW], hw[:, ki * 128:(ki + 1) * 128],
                                     hsrc[:, ki * TC: ki * TC + TCW], start=(ki == 0), stop=(ki == 31))
                        if ki == 31:
                            pr.t(m)
                pr.on('tensor', fmm)

                def fev(e, jo=jo, hdst=hdst, scoff=scoff, TCW=TCW):
                    pr.gatec(e)
                    pr.t(e.activation(hdst[:, jo * TC: jo * TC + TCW], pA[:, 0:TCW],
                                      AF.Gelu, scale=scl[:, scoff + jo:scoff + jo + 1]))
                pr.on('scalar', fev)

        for do in range(8):
            def fw(e, do=do):
                pr.gatec(e)
                for q in range(4):
                    pr.dma(e, w3b[:, q * 1024:(q + 1) * 1024].rearrange("p (kc j) -> p kc j", j=128),
                           w3[q * 1024:(q + 1) * 1024, do * 128:(do + 1) * 128]
                           .rearrange("(kc p) j -> p kc j", p=128))
            pr.on('sync', fw)

            def fmm(e, do=do, TCW=TCW):
                pr.gate(e)
                for ki in range(32):
                    m = e.matmul(pA[:, 0:TCW], w3b[:, ki * 128:(ki + 1) * 128],
                                 Ha[:, ki * TC: ki * TC + TCW], start=(ki == 0), stop=(ki == 31))
                    if ki == 31:
                        pr.t(m)
            pr.on('tensor', fmm)

            def fsc(e, do=do, TCW=TCW):
                pr.gatec(e)
                pr.t(e.tensor_scalar_mul(tmpG[:, 0:TCW], pA[:, 0:TCW], scl[:, 128 + do:129 + do]))
            pr.on('vector', fsc)
            for q in range(TCW // 128):
                def ftr(e, q=q):
                    pr.gate(e)
                    pr.t(e.transpose(pTf[:, 0:128], tmpG[:, q * 128:(q + 1) * 128], sidf[:]))
                pr.on('tensor', ftr)

                def fadd(e, do=do, q=q):
                    pr.gatec(e)
                    pr.gates(e)
                    pr.t(e.scalar_tensor_tensor(otok[:, do * 128:(do + 1) * 128],
                                                pTf[:, 0:128], 0.5,
                                                xsb[:, q * 1024 + do * 128: q * 1024 + (do + 1) * 128],
                                                ALU.mult, ALU.add))
                pr.on('vector', fadd)

                def fst(e, do=do, q=q, t0c=t0c):
                    pr.gatec(e)
                    pr.sto(e, out[t0c + q * 128: t0c + (q + 1) * 128, do * 128:(do + 1) * 128],
                           otok[:, do * 128:(do + 1) * 128])
                pr.on('gpsimd', fst)


def emit_a1(pr, nc, stack, px, io):
    xin = io['x']
    win = io['win']
    cw, cb, dtb = io['cw'], io['cb'], io['dtb']
    idb, onesf = io['idb'], io['ones']
    z_out = io['z']
    dt_out = io['dtstk']
    xh_out = io['xhstk']
    bct_out = io['bct']
    bcf_out = io['bcf']
    NT = TA // 128
    TGS = [(g * 512, 512) for g in range(TA // 512)]
    if TA % 512:
        TGS.append((TA - TA % 512, TA % 512))
    sbt = lambda n, shp, dt: stack.enter_context(nc.sbuf_tensor(px + n, shp, dt))
    ps = lambda n, shp: stack.enter_context(nc.psum_tensor(px + n, shp, F32))
    kq = mkconsts(pr, stack, nc, [1e-5], px + "kq")
    xt = sbt("xt", [128, 1024], F32)
    xlt = sbt("xlt", [128, 1024], BF16)
    sq = sbt("sq", [128, 1024], F32)
    cols = sbt("cols", [128, 16], F32)
    sid = sbt("sid", [128, 128], BF16)
    sones = sbt("sones", [1, 128], F32)
    xf = sbt("xf", [128, 8 * TA], BF16)
    wz = sbt("wz", [128, 8 * 512], BF16)
    wdt = sbt("wdt", [128, 8 * 32], BF16)
    dtbs = sbt("dtbs", [1, 32], F32)
    dtbb = sbt("dtbb", [128, 32], F32)
    cwsb = sbt("cwsb", [128, 17 * 4], F32)
    cbsb = sbt("cbsb", [128, 17], F32)
    strip = sbt("strip", [128, TA], F32)
    acc = sbt("acc", [128, TA], F32)
    sbf = sbt("sbf", [128, TA], BF16)
    ztmp = sbt("ztmp", [128, 512], F32)
    dtt = sbt("dtt", [128, 32 * 5], F32)
    pZ = ps("pZ", [128, 512])
    pC = ps("pC", [128, 512])
    pD2 = ps("pD2", [128, 32])
    pB2 = ps("pB2", [128, 32])
    pT = stack.enter_context(nc.psum_tensor(px + "pT", [128, 128], BF16))

    def f0(e):
        pr.dma(e, sid[:], idb)
        pr.dma(e, sones[:], onesf)
        pr.dma(e, cwsb[:], cw)
        pr.dma(e, cbsb[:], cb)
        pr.dma(e, dtbs[:], dtb)
        pr.dma(e, wdt[:].rearrange("p (kc j) -> p kc j", j=32),
               win[:, 4128:4160].rearrange("(kc p) j -> p kc j", p=128))
    pr.on('sync', f0)

    def f1(e):
        pr.gate(e)
        pr.t(e.matmul(pB2[:, 0:32], sones[:], dtbs[:], start=True, stop=True))
    pr.on('tensor', f1)

    def f1b(e):
        pr.gatec(e)
        pr.t(e.copy(dtbb[:], pB2[:, 0:32]))
    pr.on('scalar', f1b)

    for r in range(NT):
        def fl(e, r=r):
            pr.gatec(e)
            pr.dma(e, xt[:], xin[r * 128:(r + 1) * 128, :])
        pr.on('sync', fl)
        ln_tile(pr, xt[:], xlt[:], {'sq': sq, 'cols': cols, 'eps': kq[1e-5]})
        for kc2 in range(8):
            def ftr(e, kc2=kc2):
                pr.gate(e)
                pr.t(e.transpose(pT[:, 0:128], xlt[:, kc2 * 128:(kc2 + 1) * 128], sid[:]))
            pr.on('tensor', ftr)

            def fcp(e, kc2=kc2, r=r):
                pr.gatec(e)
                pr.t(e.copy(xf[:, kc2 * TA + r * 128: kc2 * TA + (r + 1) * 128], pT[:, 0:128]))
            pr.on('scalar', fcp)

        def fdt(e, r=r):
            pr.gatec(e)
            for ki in range(8):
                m = e.matmul(pD2[:, 0:32], xf[:, ki * TA + r * 128: ki * TA + (r + 1) * 128],
                             wdt[:, ki * 32:(ki + 1) * 32], start=(ki == 0), stop=(ki == 7))
                if ki == 7:
                    pr.t(m)
        pr.on('tensor', fdt)

        def fsp(e):
            pr.gatec(e)
            pr.t(e.tensor_add(dtt[:, 0:32], pD2[:, 0:32], dtbb[:]))
        pr.on('vector', fsp)

        def fsp2(e):
            pr.gatec(e)
            pr.t(e.activation(dtt[:, 32:64], dtt[:, 0:32], AF.Abs))
            pr.gatec(e)
            pr.t(e.activation(dtt[:, 32:64], dtt[:, 32:64], AF.Exp, scale=-1.0))
            pr.gatec(e)
            pr.t(e.activation(dtt[:, 32:64], dtt[:, 32:64], AF.Ln, bias=1.0))
        pr.on('scalar', fsp2)

        def fsp3(e):
            pr.gatec(e)
            pr.t(e.tensor_scalar_max(dtt[:, 64:96], dtt[:, 0:32], 0.0))
            pr.gatec(e)
            pr.t(e.tensor_add(dtt[:, 96:128], dtt[:, 32:64], dtt[:, 64:96]))
        pr.on('vector', fsp3)

        def fstd(e, r=r):
            pr.gatec(e)
            pr.sto(e, dt_out[r * 128:(r + 1) * 128, :], dtt[:, 96:112])
            pr.sto(e, dt_out[TA + r * 128: TA + (r + 1) * 128, :], dtt[:, 112:128])
        pr.on('gpsimd', fstd)

    for zb in range(4):
        def fwz(e, zb=zb):
            pr.gatec(e)
            pr.dma(e, wz[:].rearrange("p (kc j) -> p kc j", j=512),
                   win[:, zb * 512:(zb + 1) * 512].rearrange("(kc p) j -> p kc j", p=128))
        pr.on('sync', fwz)
        for r in range(NT):
            def fmm(e, r=r):
                pr.gate(e)
                for ki in range(8):
                    m = e.matmul(pZ[:, 0:512], xf[:, ki * TA + r * 128: ki * TA + (r + 1) * 128],
                                 wz[:, ki * 512:(ki + 1) * 512], start=(ki == 0), stop=(ki == 7))
                    if ki == 7:
                        pr.t(m)
            pr.on('tensor', fmm)

            def fev(e):
                pr.gatec(e)
                pr.gates(e)
                pr.t(e.copy(ztmp[:], pZ[:, 0:512]))
            pr.on('scalar', fev)

            def fst(e, r=r, zb=zb):
                pr.gatec(e)
                pr.sto(e, z_out[r * 128:(r + 1) * 128, zb * 512:(zb + 1) * 512], ztmp[:])
            pr.on('gpsimd', fst)

    for cbk in range(17):
        ncr = 128 if cbk < 16 else 32

        def fwc(e, cbk=cbk, ncr=ncr):
            pr.gatec(e)
            pr.dma(e, wz[:].rearrange("p (kc j) -> p kc j", j=512)[:, :, 0:ncr],
                   win[:, 2048 + cbk * 128: 2048 + cbk * 128 + ncr]
                   .rearrange("(kc p) j -> p kc j", p=128))
        pr.on('sync', fwc)
        for (o, w) in TGS:
            def fmm(e, o=o, w=w, ncr=ncr):
                pr.gate(e)
                for ki in range(8):
                    m = e.matmul(pC[0:ncr, 0:w], wz[:, ki * 512:ki * 512 + ncr],
                                 xf[:, ki * TA + o: ki * TA + o + w],
                                 start=(ki == 0), stop=(ki == 7))
                    if ki == 7:
                        pr.t(m)
            pr.on('tensor', fmm)

            def fev(e, o=o, w=w, ncr=ncr):
                pr.gatec(e)
                pr.t(e.copy(strip[0:ncr, o:o + w], pC[0:ncr, 0:w]))
            pr.on('scalar', fev)
        W = TA - 3

        def fcv(e, cbk=cbk):
            pr.gatec(e)
            pr.t(e.tensor_scalar(acc[:, 3:TA], strip[:, 0:W], cwsb[:, cbk * 4:cbk * 4 + 1], None, ALU.mult))
            for j in range(1, 4):
                pr.gatec(e)
                pr.t(e.scalar_tensor_tensor(acc[:, 3:TA], strip[:, j:j + W],
                                            cwsb[:, cbk * 4 + j:cbk * 4 + j + 1],
                                            acc[:, 3:TA], ALU.mult, ALU.add))
        pr.on('vector', fcv)

        def fsl(e, cbk=cbk):
            pr.gatec(e)
            pr.t(e.activation(sbf[:, 3:TA], acc[:, 3:TA], AF.Silu, bias=cbsb[:, cbk:cbk + 1]))
        pr.on('scalar', fsl)

        if cbk < 16:
            for r in range(NT):
                def ftr(e, r=r):
                    pr.gate(e)
                    pr.t(e.transpose(pT[:, 0:128], sbf[:, r * 128:(r + 1) * 128], sid[:]))
                pr.on('tensor', ftr)

                def fcp(e):
                    pr.gatec(e)
                    pr.gates(e)
                    pr.t(e.copy(xlt[:, 0:128], pT[:, 0:128]))
                pr.on('scalar', fcp)

                def fst(e, r=r, cbk=cbk):
                    pr.gatec(e)
                    j = cbk // 8
                    pr.sto(e, xh_out[j * TA + r * 128: j * TA + (r + 1) * 128,
                                     (cbk % 8) * 128:(cbk % 8 + 1) * 128], xlt[:, 0:128])
                pr.on('gpsimd', fst)
        else:
            def fstf(e):
                pr.gatec(e)
                pr.sto(e, bcf_out[:, 3:TA], sbf[0:32, 3:TA])
            pr.on('gpsimd', fstf)
            for r in range(NT):
                def ftr(e, r=r):
                    pr.gate(e)
                    pr.t(e.transpose(pT[0:128, 0:32], sbf[0:32, r * 128:(r + 1) * 128], sid[0:32, 0:32]))
                pr.on('tensor', ftr)

                def fcp(e):
                    pr.gatec(e)
                    pr.gates(e)
                    pr.t(e.copy(xlt[:, 0:32], pT[:, 0:32]))
                pr.on('scalar', fcp)

                def fst(e, r=r):
                    pr.gatec(e)
                    pr.sto(e, bct_out[r * 128:(r + 1) * 128, :], xlt[:, 0:32])
                pr.on('gpsimd', fst)


def emit_a2(pr, nc, stack, px, io):
    SQ = SEQ
    NCH = SQ // 64
    xh_g, bct_g, bcf_g, dt_g = io['xh_g'], io['bct_g'], io['bcf_g'], io['dt_g']
    tA2 = io['tA2']
    alog, dv = io['alog'], io['dv']
    triu, mneg, sel, onesf, idb, idf = io['triu'], io['mneg'], io['sel'], io['ones'], io['idb'], io['idf']
    y_out = io['ystk']
    sbt = lambda n, shp, dt_: stack.enter_context(nc.sbuf_tensor(px + n, shp, dt_))
    ps = lambda n, shp: stack.enter_context(nc.psum_tensor(px + n, shp, F32))
    I32 = mybir.dt.int32
    struw = sbt("struw", [64, 64], F32)
    smneg = sbt("smneg", [64, 64], F32)
    ssel = sbt("ssel", [16, 16 * 64], F32)
    sones = sbt("sones", [1, 128], F32)
    sid = sbt("sid", [128, 128], BF16)
    sidf = sbt("sidf", [128, 128], F32)
    salog = sbt("salog", [1, 16], F32)
    sdv = sbt("sdv", [1, 16], F32)
    arow = sbt("arow", [1, 16], F32)
    A_bc = sbt("A_bc", [64, 16], F32)
    D_bc = sbt("D_bc", [64, 16], F32)
    sidx = sbt("sidx", [64, 128], I32)
    xhc = sbt("xhc", [64, 1024], BF16)
    xcd = sbt("xcd", [64, 1024], BF16)
    bctc = sbt("bctc", [64, 32], BF16)
    bfm = sbt("bfm", [16, 64], BF16)
    cfm = sbt("cfm", [16, 64], BF16)
    dtc = sbt("dtc", [64, 16], F32)
    a_sb = sbt("a_sb", [64, 16], F32)
    cs_sb = sbt("cs_sb", [64, 16], F32)
    tot_sb = sbt("tot_sb", [1, 16], F32)
    etot = sbt("etot", [1, 16], F32)
    exp_cs = sbt("exp_cs", [64, 16], F32)
    decay = sbt("decay", [64, 16], F32)
    drep = sbt("drep", [16, 16], F32)
    csT = sbt("csT", [16, 64], F32)
    Gt_sb = sbt("Gt_sb", [64, 64], F32)
    diff = sbt("diff", [64, 64], F32)
    Lt = sbt("Lt", [64, 64], F32)
    Ph = sbt("Ph", [64, 64], BF16)
    Ccs = sbt("Ccs", [64, 16], BF16)
    cctS = sbt("cctS", [16, 64], BF16)
    Bcs = sbt("Bcs", [64, 16], BF16)
    R_f = sbt("R_f", [16, 1024], F32)
    R_b = sbt("R_b", [16, 1024], BF16)
    y_sb = sbt("y_sb", [64, 1024], F32)
    zt = sbt("zt", [128, 1024], F32)
    ones64 = sbt("ones64", [64, 1], F32)
    pP = ps("pP", [64, 64])
    pGt = ps("pGt", [64, 64])
    pBC = ps("pBC", [64, 64])
    pY = ps("pY", [64, 64])
    pS = ps("pS", [16, 64])
    pTb = stack.enter_context(nc.psum_tensor(px + "pTb", [16, 64], BF16))

    def f0(e):
        pr.dma(e, struw[:], triu)
        pr.dma(e, smneg[:], mneg)
        pr.dma(e, ssel[:], sel)
        pr.dma(e, sones[:], onesf)
        pr.dma(e, sid[:], idb)
        pr.dma(e, sidf[:], idf)
        pr.dma(e, salog[:], alog)
        pr.dma(e, sdv[:], dv)
        pr.dma(e, sidx[:], tA2)
    pr.on('sync', f0)

    def fz(e):
        pr.gate(e)
        pr.t(e.memset(R_f[:], 0.0))
        pr.gatec(e)
        pr.t(e.memset(R_b[:], 0.0))
        pr.gatec(e)
        pr.t(e.memset(zt[:], 0.0))
        pr.gatec(e)
        pr.t(e.memset(ones64[:], 1.0))
    pr.on('vector', fz)

    def fzh(e):
        pr.gatec(e)
        pr.sto(e, y_out[0:128, :], zt[:])
    pr.on('gpsimd', fzh)

    def fa(e):
        pr.gate(e)
        pr.t(e.activation(arow[:], salog[:], AF.Exp))
    pr.on('scalar', fa)

    def fa2(e):
        pr.gatec(e)
        pr.t(e.tensor_scalar_mul(arow[:], arow[:], -1.0))
    pr.on('vector', fa2)

    def fa3(e):
        pr.gatec(e)
        pr.t(e.matmul(pP[:, 0:16], sones[0:1, 0:64], arow[:], start=True, stop=True))
    pr.on('tensor', fa3)

    def fa4(e):
        pr.gatec(e)
        pr.t(e.copy(A_bc[:], pP[:, 0:16]))
    pr.on('scalar', fa4)

    def fa5(e):
        pr.gatec(e)
        pr.t(e.matmul(pP[:, 0:16], sones[0:1, 0:64], sdv[:], start=True, stop=True))
    pr.on('tensor', fa5)

    def fa6(e):
        pr.gatec(e)
        pr.t(e.copy(D_bc[:], pP[:, 0:16]))
    pr.on('scalar', fa6)

    for k in range(NCH):
        l = k * 64
        half, lr = l // 4096, l % 4096

        def fl(e, k=k, half=half, lr=lr):
            pr.gatec(e)
            pr.gates(e)
            e.indirect_dma_start(xhc[:], None, xh_g, IOA(sidx[:, k:k + 1], 0)).then_inc(pr.ld, 16)
            pr.l += 16
            e.indirect_dma_start(dtc[:], None, dt_g, IOA(sidx[:, k:k + 1], 0)).then_inc(pr.ld, 16)
            pr.l += 16
            pr.dma(e, bctc[:], bct_g[half * TA + 128 + lr: half * TA + 128 + lr + 64, :])
            pr.dma(e, bfm[:], bcf_g[half * 32: half * 32 + 16, 128 + lr: 128 + lr + 64])
            pr.dma(e, cfm[:], bcf_g[half * 32 + 16: half * 32 + 32, 128 + lr: 128 + lr + 64])
        pr.on('gpsimd', fl)

        def fp1(e):
            pr.gate(e)
            pr.t(e.tensor_mul(a_sb[:], dtc[:], A_bc[:]))
        pr.on('vector', fp1)

        def fp2(e):
            pr.gatec(e)
            pr.t(e.matmul(pP[:, 0:16], struw[:], a_sb[:], start=True, stop=True))
            pr.gatec(e)
            pr.t(e.matmul(pBC[0:1, 0:16], ones64[:], a_sb[:], start=True, stop=True))
        pr.on('tensor', fp2)

        def fp3(e):
            pr.gatec(e)
            pr.t(e.copy(cs_sb[:], pP[:, 0:16]))
            pr.gatec(e)
            pr.t(e.copy(tot_sb[:], pBC[0:1, 0:16]))
            pr.gatec(e)
            pr.t(e.activation(exp_cs[:], cs_sb[:], AF.Exp))
            pr.gatec(e)
            pr.t(e.activation(etot[:], tot_sb[:], AF.Exp))
        pr.on('scalar', fp3)

        def fp4(e):
            pr.gatec(e)
            pr.t(e.matmul(pP[:, 0:16], sones[0:1, 0:64], tot_sb[:], start=True, stop=True))
            pr.gatec(e)
            pr.t(e.matmul(pBC[0:16, 0:16], sones[0:1, 0:16], etot[:], start=True, stop=True))
        pr.on('tensor', fp4)

        def fp5(e):
            pr.gatec(e)
            pr.t(e.tensor_sub(decay[:], pP[:, 0:16], cs_sb[:]))
            pr.gatec(e)
            pr.t(e.tensor_copy(drep[:], pBC[0:16, 0:16]))
        pr.on('vector', fp5)

        def fp6(e):
            pr.gatec(e)
            pr.t(e.activation(decay[:], decay[:], AF.Exp))
        pr.on('scalar', fp6)

        def fp7(e):
            pr.gatec(e)
            pr.t(e.transpose(pP[0:16, 0:64], cs_sb[:], sidf[0:64, 0:64]))
        pr.on('tensor', fp7)

        def fp8(e):
            pr.gatec(e)
            pr.t(e.copy(csT[:], pP[0:16, 0:64]))
        pr.on('scalar', fp8)

        def fp9(e):
            pr.gatec(e)
            pr.t(e.matmul(pGt[:, 0:64], bfm[:], cfm[:], start=True, stop=True))
        pr.on('tensor', fp9)

        def fp10(e):
            pr.gatec(e)
            pr.t(e.copy(Gt_sb[:], pGt[:, 0:64]))
        pr.on('scalar', fp10)

        for h in range(16):
            def g1(e, h=h):
                pr.gatec(e)
                pr.t(e.tensor_scalar(xcd[:, h * 64:(h + 1) * 64], xhc[:, h * 64:(h + 1) * 64],
                                     dtc[:, h:h + 1], None, ALU.mult))
                pr.gatec(e)
                pr.t(e.tensor_scalar(Ccs[:], bctc[:, 16:32], exp_cs[:, h:h + 1], None, ALU.mult))
                pr.gatec(e)
                pr.t(e.tensor_scalar(Bcs[:], bctc[:, 0:16], decay[:, h:h + 1], None, ALU.mult))
            pr.on('vector', g1)

            def g2(e, h=h):
                pr.gatec(e)
                pr.t(e.matmul(pBC[:, 0:64], ssel[:, h * 64:(h + 1) * 64], csT[:], start=True, stop=True))
                pr.gatec(e)
                pr.t(e.transpose(pTb[:, 0:64], Ccs[:], sid[0:64, 0:64]))
            pr.on('tensor', g2)

            def g3(e, h=h):
                pr.gatec(e)
                pr.t(e.scalar_tensor_tensor(diff[:], pBC[:, 0:64], cs_sb[:, h:h + 1], smneg[:],
                                            ALU.subtract, ALU.add))
            pr.on('vector', g3)

            def g4(e, h=h):
                pr.gatec(e)
                pr.t(e.activation(Lt[:], diff[:], AF.Exp))
                pr.gatec(e)
                pr.t(e.copy(cctS[:], pTb[:, 0:64]))
            pr.on('scalar', g4)

            def g5(e, h=h):
                pr.gatec(e)
                pr.t(e.tensor_mul(Ph[:], Lt[:], Gt_sb[:]))
            pr.on('vector', g5)

            def g6(e, h=h):
                pr.gatec(e)
                e.matmul(pY[:, 0:64], Ph[:], xcd[:, h * 64:(h + 1) * 64], start=True, stop=False)
                pr.t(e.matmul(pY[:, 0:64], cctS[:], R_b[:, h * 64:(h + 1) * 64], start=False, stop=True))
                pr.gatec(e)
                pr.t(e.matmul(pS[:, 0:64], Bcs[:], xcd[:, h * 64:(h + 1) * 64], start=True, stop=True))
            pr.on('tensor', g6)

            def g7(e, h=h):
                pr.gatec(e)
                pr.t(e.scalar_tensor_tensor(y_sb[:, h * 64:(h + 1) * 64], xhc[:, h * 64:(h + 1) * 64],
                                            D_bc[:, h:h + 1], pY[:, 0:64], ALU.mult, ALU.add))
                pr.gatec(e)
                pr.t(e.scalar_tensor_tensor(R_f[:, h * 64:(h + 1) * 64], R_f[:, h * 64:(h + 1) * 64],
                                            drep[:, h:h + 1], pS[:, 0:64], ALU.mult, ALU.add))
            pr.on('vector', g7)

            def g8(e, h=h):
                pr.gatec(e)
                pr.t(e.copy(R_b[:, h * 64:(h + 1) * 64], R_f[:, h * 64:(h + 1) * 64]))
            pr.on('scalar', g8)

        def fst(e, k=k, half=half, lr=lr):
            pr.gatec(e)
            pr.sto(e, y_out[half * TA + 128 + lr: half * TA + 128 + lr + 64, :], y_sb[:])
            if 3968 <= l < 4096:
                pr.sto(e, y_out[TA + (l - 3968): TA + (l - 3968) + 64, :], y_sb[:])
        pr.on('gpsimd', fst)


def emit_a3(pr, nc, stack, px, io):
    y_g = io['y_g']
    tA3 = io['tA3']
    zin, xres, out = io['z'], io['xr'], io['out']
    wo, nw = io['wo'], io['nw']
    idb, idf, onesf = io['idb'], io['idf'], io['ones']
    groups = io['groups']
    I32 = mybir.dt.int32
    sbt = lambda n, shp, dt_: stack.enter_context(nc.sbuf_tensor(px + n, shp, dt_))
    ps = lambda n, shp: stack.enter_context(nc.psum_tensor(px + n, shp, F32))
    kq = mkconsts(pr, stack, nc, [1e-5], px + "kq")
    yt = sbt("yt", [128, 2048], F32)
    zt = sbt("zt", [128, 2048], F32)
    xt = sbt("xt", [128, 1024], F32)
    g_ = sbt("g", [128, 2048], F32)
    sq = sbt("sq", [128, 2048], F32)
    cols = sbt("cols", [128, 8], F32)
    sid = sbt("sid", [128, 128], BF16)
    sidf = sbt("sidf", [128, 128], F32)
    sones = sbt("sones", [1, 128], F32)
    nwr = sbt("nwr", [1, 2048], F32)
    nwb = sbt("nwb", [128, 2048], F32)
    gb = sbt("gb", [128, 2048], BF16)
    gf = sbt("gf", [128, 16 * 512], BF16)
    wos = sbt("wos", [128, 16 * 128], BF16)
    tmp = sbt("tmp", [128, 512], F32)
    otok = sbt("otok", [128, 4 * 1024], F32)
    sidx = sbt("sidx", [128, 66], I32)
    pA = ps("pA", [128, 512])
    pTf = ps("pTf", [128, 128])
    pT = stack.enter_context(nc.psum_tensor(px + "pT", [128, 128], BF16))

    def f0(e):
        pr.dma(e, sid[:], idb)
        pr.dma(e, sidf[:], idf)
        pr.dma(e, sones[:], onesf)
        pr.dma(e, nwr[:], nw)
        pr.dma(e, sidx[:], tA3)
    pr.on('sync', f0)

    for q in range(4):
        def fb(e, q=q):
            pr.gate(e)
            pr.t(e.matmul(pA[:, 0:512], sones[:], nwr[:, q * 512:(q + 1) * 512], start=True, stop=True))
        pr.on('tensor', fb)

        def fb2(e, q=q):
            pr.gatec(e)
            pr.t(e.copy(nwb[:, q * 512:(q + 1) * 512], pA[:, 0:512]))
        pr.on('scalar', fb2)

    for (t0g, wg) in groups:
        NR4 = wg // 128
        for r4 in range(NR4):
            r = t0g // 128 + r4

            def fl(e, r=r):
                pr.gatec(e)
                e.indirect_dma_start(yt[:, 0:1024], None, y_g, IOA(sidx[:, 2 * r:2 * r + 1], 0)).then_inc(pr.ld, 16)
                pr.l += 16
                e.indirect_dma_start(yt[:, 1024:2048], None, y_g, IOA(sidx[:, 2 * r + 1:2 * r + 2], 0)).then_inc(pr.ld, 16)
                pr.l += 16
                pr.dma(e, zt[:], zin[r * 128:(r + 1) * 128, :])
            pr.on('gpsimd', fl)

            def fz(e):
                pr.gate(e)
                pr.t(e.activation(sq[:], zt[:], AF.Silu))
            pr.on('scalar', fz)

            def fg(e):
                pr.gatec(e)
                pr.t(e.tensor_mul(g_[:], yt[:], sq[:]))
            pr.on('vector', fg)

            def fsq(e):
                pr.gatec(e)
                pr.t(e.activation(sq[:], g_[:], AF.Square))
            pr.on('scalar', fsq)

            def fms(e):
                pr.gatec(e)
                pr.t(e.reduce_sum(cols[:, 0:1], sq[:], AX.X))
                pr.gatec(e)
                pr.t(e.tensor_scalar_mul(cols[:, 0:1], cols[:, 0:1], 1.0 / 2048))
            pr.on('vector', fms)

            def frs(e):
                pr.gatec(e)
                pr.t(e.activation(cols[:, 1:2], cols[:, 0:1], AF.Ln, bias=kq[1e-5]))
                pr.gatec(e)
                pr.t(e.activation(cols[:, 1:2], cols[:, 1:2], AF.Exp, scale=-0.5))
            pr.on('scalar', frs)

            def fnw(e):
                pr.gatec(e)
                pr.t(e.tensor_scalar(g_[:], g_[:], cols[:, 1:2], None, ALU.mult))
                pr.gatec(e)
                pr.t(e.tensor_mul(gb[:], g_[:], nwb[:]))
            pr.on('vector', fnw)

            for kc2 in range(16):
                def ftr(e, kc2=kc2):
                    pr.gate(e)
                    pr.t(e.transpose(pT[:, 0:128], gb[:, kc2 * 128:(kc2 + 1) * 128], sid[:]))
                pr.on('tensor', ftr)

                def fcp(e, kc2=kc2, r4=r4):
                    pr.gatec(e)
                    pr.t(e.copy(gf[:, kc2 * 512 + r4 * 128: kc2 * 512 + (r4 + 1) * 128], pT[:, 0:128]))
                pr.on('scalar', fcp)

        for do in range(8):
            def fw(e, do=do):
                pr.gatec(e)
                pr.dma(e, wos[:].rearrange("p (kc j) -> p kc j", j=128),
                       wo[:, do * 128:(do + 1) * 128].rearrange("(kc p) j -> p kc j", p=128))
            pr.on('sync', fw)

            def fmm(e, do=do, wg=wg):
                pr.gate(e)
                for ki in range(16):
                    m = e.matmul(pA[:, 0:wg], wos[:, ki * 128:(ki + 1) * 128],
                                 gf[:, ki * 512: ki * 512 + wg], start=(ki == 0), stop=(ki == 15))
                    if ki == 15:
                        pr.t(m)
            pr.on('tensor', fmm)

            def fev(e, wg=wg):
                pr.gatec(e)
                pr.t(e.copy(tmp[:, 0:wg], pA[:, 0:wg]))
            pr.on('scalar', fev)

            for q in range(wg // 128):
                def ftr(e, q=q):
                    pr.gate(e)
                    pr.t(e.transpose(pTf[:, 0:128], tmp[:, q * 128:(q + 1) * 128], sidf[:]))
                pr.on('tensor', ftr)

                def fcp2(e, q=q, do=do):
                    pr.gatec(e)
                    pr.gates(e)
                    pr.t(e.tensor_copy(otok[:, q * 1024 + do * 128: q * 1024 + (do + 1) * 128], pTf[:, 0:128]))
                pr.on('vector', fcp2)

        for q in range(wg // 128):
            r = t0g // 128 + q

            def flx(e, r=r):
                pr.gatec(e)
                pr.dma(e, xt[:], xres[r * 128:(r + 1) * 128, :])
            pr.on('sync', flx)

            def fadd(e, q=q):
                pr.gate(e)
                pr.t(e.tensor_add(otok[:, q * 1024:(q + 1) * 1024], otok[:, q * 1024:(q + 1) * 1024], xt[:]))
            pr.on('vector', fadd)

            def fst(e, r=r, q=q):
                pr.gatec(e)
                pr.sto(e, out[r * 128:(r + 1) * 128, :], otok[:, q * 1024:(q + 1) * 1024])
            pr.on('gpsimd', fst)


def emit_b1(pr, nc, stack, px, io):
    xin = io['x']
    wq, wk, wv, wbv = io['wq'], io['wk'], io['wv'], io['wbv']
    cq, ck, cv, idb = io['cq'], io['ck'], io['cv'], io['idb']
    qf_o, kf_o, vf_o = io['qf'], io['kf'], io['vf']
    qt_o, kt_o, vt_o = io['qtstk'], io['ktstk'], io['vtstk']
    vtl_o = io['vtl']
    bt_o = io['btstk']
    NT = TA // 128
    TGS = [(g * 512, 512) for g in range(TA // 512)]
    if TA % 512:
        TGS.append((TA - TA % 512, TA % 512))
    sbt = lambda n, shp, dt_: stack.enter_context(nc.sbuf_tensor(px + n, shp, dt_))
    ps = lambda n, shp: stack.enter_context(nc.psum_tensor(px + n, shp, F32))
    kq = mkconsts(pr, stack, nc, [1e-5], px + "kq")
    xt = sbt("xt", [128, 1024], F32)
    xlt = sbt("xlt", [128, 1024], BF16)
    sq = sbt("sq", [128, 1024], F32)
    cols = sbt("cols", [128, 16], F32)
    sid = sbt("sid", [128, 128], BF16)
    xf = sbt("xf", [128, 8 * TA], BF16)
    wsb = sbt("wsb", [128, 8 * 128], BF16)
    wbsb = sbt("wbsb", [128, 8 * 4], BF16)
    cwsb = sbt("cwsb", [128, 3 * 32], F32)
    strip = sbt("strip", [128, TA], F32)
    acc = sbt("acc", [128, TA], F32)
    sbf = sbt("sbf", [128, TA], BF16)
    ttile = sbt("ttile", [128, 128], BF16)
    btile = sbt("btile", [128, 4], F32)
    pC = ps("pC", [128, 512])
    pB2 = ps("pB2", [128, 4])
    pT = stack.enter_context(nc.psum_tensor(px + "pT", [128, 128], BF16))

    def f0(e):
        pr.dma(e, sid[:], idb)
        pr.dma(e, cwsb[:, 0:32], cq)
        pr.dma(e, cwsb[:, 32:64], ck)
        pr.dma(e, cwsb[:, 64:96], cv)
        pr.dma(e, wbsb[:].rearrange("p (kc j) -> p kc j", j=4),
               wbv.rearrange("(kc p) j -> p kc j", p=128))
    pr.on('sync', f0)

    for r in range(NT):
        def fl(e, r=r):
            pr.gatec(e)
            pr.dma(e, xt[:], xin[r * 128:(r + 1) * 128, :])
        pr.on('sync', fl)
        ln_tile(pr, xt[:], xlt[:], {'sq': sq, 'cols': cols, 'eps': kq[1e-5]})
        for kc2 in range(8):
            def ftr(e, kc2=kc2):
                pr.gate(e)
                pr.t(e.transpose(pT[:, 0:128], xlt[:, kc2 * 128:(kc2 + 1) * 128], sid[:]))
            pr.on('tensor', ftr)

            def fcp(e, kc2=kc2, r=r):
                pr.gatec(e)
                pr.t(e.copy(xf[:, kc2 * TA + r * 128: kc2 * TA + (r + 1) * 128], pT[:, 0:128]))
            pr.on('scalar', fcp)

        def fbm(e, r=r):
            pr.gatec(e)
            for ki in range(8):
                m = e.matmul(pB2[:, 0:4], xf[:, ki * TA + r * 128: ki * TA + (r + 1) * 128],
                             wbsb[:, ki * 4:(ki + 1) * 4], start=(ki == 0), stop=(ki == 7))
                if ki == 7:
                    pr.t(m)
        pr.on('tensor', fbm)

        def fbs(e):
            pr.gatec(e)
            pr.gates(e)
            pr.t(e.activation(btile[:], pB2[:, 0:4], AF.Sigmoid))
        pr.on('scalar', fbs)

        def fbst(e, r=r):
            pr.gatec(e)
            pr.sto(e, bt_o[r * 128:(r + 1) * 128, :], btile[:, 0:2])
            pr.sto(e, bt_o[TA + r * 128: TA + (r + 1) * 128, :], btile[:, 2:4])
        pr.on('gpsimd', fbst)

    for (wmat, cvoff, fo, to, tl) in ((wq, 0, qf_o, qt_o, None), (wk, 32, kf_o, kt_o, None),
                                      (wv, 64, vf_o, vt_o, vtl_o)):
        for cbk in range(8):
            def fwc(e, cbk=cbk, wmat=wmat):
                pr.gatec(e)
                pr.dma(e, wsb[:].rearrange("p (kc j) -> p kc j", j=128),
                       wmat[:, cbk * 128:(cbk + 1) * 128].rearrange("(kc p) j -> p kc j", p=128))
            pr.on('sync', fwc)
            for (o, w) in TGS:
                def fmm(e, cbk=cbk, o=o, w=w):
                    pr.gate(e)
                    for ki in range(8):
                        m = e.matmul(pC[:, 0:w], wsb[:, ki * 128:(ki + 1) * 128],
                                     xf[:, ki * TA + o: ki * TA + o + w],
                                     start=(ki == 0), stop=(ki == 7))
                        if ki == 7:
                            pr.t(m)
                pr.on('tensor', fmm)

                def fev(e, o=o, w=w):
                    pr.gatec(e)
                    pr.t(e.copy(strip[:, o:o + w], pC[:, 0:w]))
                pr.on('scalar', fev)
            W = TA - 3

            def fcv(e, cbk=cbk, cvoff=cvoff):
                pr.gatec(e)
                pr.t(e.tensor_scalar(acc[:, 3:TA], strip[:, 0:W],
                                     cwsb[:, cvoff + cbk * 4:cvoff + cbk * 4 + 1], None, ALU.mult))
                for j in range(1, 4):
                    pr.gatec(e)
                    pr.t(e.scalar_tensor_tensor(acc[:, 3:TA], strip[:, j:j + W],
                                                cwsb[:, cvoff + cbk * 4 + j:cvoff + cbk * 4 + j + 1],
                                                acc[:, 3:TA], ALU.mult, ALU.add))
            pr.on('vector', fcv)

            def fsl(e):
                pr.gatec(e)
                pr.gates(e)
                pr.t(e.activation(sbf[:, 3:TA], acc[:, 3:TA], AF.Silu))
            pr.on('scalar', fsl)

            def fstf(e, cbk=cbk, fo=fo):
                pr.gatec(e)
                pr.sto(e, fo[cbk * 128:(cbk + 1) * 128, 3:TA], sbf[:, 3:TA])
            pr.on('gpsimd', fstf)
            for r in range(NT):
                def ftr(e, r=r):
                    pr.gate(e)
                    pr.t(e.transpose(pT[:, 0:128], sbf[:, r * 128:(r + 1) * 128], sid[:]))
                pr.on('tensor', ftr)

                def fcp(e):
                    pr.gatec(e)
                    pr.gates(e)
                    pr.t(e.copy(ttile[:], pT[:, 0:128]))
                pr.on('scalar', fcp)

                def fst(e, r=r, cbk=cbk, to=to, tl=tl):
                    pr.gatec(e)
                    j = cbk // 4
                    pr.sto(e, to[j * TA + r * 128: j * TA + (r + 1) * 128,
                                 (cbk % 4) * 128:(cbk % 4 + 1) * 128], ttile[:])
                    if tl is not None:
                        pr.sto(e, tl[r * 128:(r + 1) * 128, cbk * 128:(cbk + 1) * 128], ttile[:])
                pr.on('gpsimd', fst)


def emit_b2(pr, nc, stack, px, io):
    SQ = SEQ
    NCH = SQ // 32
    qf_g, kf_g = io['qf_g'], io['kf_g']
    qt_g, kt_g, vt_g = io['qt_g'], io['kt_g'], io['vt_g']
    bt_g = io['bt_g']
    tqf, tqt = io['tqf'], io['tqt']
    msl, msu, mud, id32 = io['msl'], io['msu'], io['mud'], io['id32']
    idf, idb, onesf = io['idf'], io['idb'], io['ones']
    dout = io['doutstk']
    I32 = mybir.dt.int32
    sbt = lambda n, shp, dt_: stack.enter_context(nc.sbuf_tensor(px + n, shp, dt_))
    ps = lambda n, shp: stack.enter_context(nc.psum_tensor(px + n, shp, F32))
    kq = mkconsts(pr, stack, nc, [1e-6], px + "kq")
    smsl = sbt("smsl", [32, 32], F32)
    smsu = sbt("smsu", [32, 32], F32)
    smud = sbt("smud", [32, 32], F32)
    sid32 = sbt("sid32", [32, 32], F32)
    s2i = sbt("s2i", [32, 32], F32)
    sidf = sbt("sidf", [128, 128], F32)
    sid = sbt("sid", [128, 128], BF16)
    sones = sbt("sones", [1, 128], F32)
    sixf = sbt("sixf", [128, 8], I32)
    sixt = sbt("sixt", [32, 256], I32)
    qtc = sbt("qtc", [32, 256], BF16)
    ktc = sbt("ktc", [32, 256], BF16)
    vtc = sbt("vtc", [32, 256], BF16)
    qfs = sbt("qfs", [128, 64], BF16)
    kfs = sbt("kfs", [128, 64], BF16)
    betac = sbt("betac", [32, 2], F32)
    sq2 = sbt("sq2", [32, 256], F32)
    ncol = sbt("ncol", [32, 8], F32)
    rowb = sbt("rowb", [1, 32], F32)
    mbc1 = sbt("mbc1", [32, 32], F32)
    mbc2 = sbt("mbc2", [32, 32], F32)
    Nsb = sbt("Nsb", [32, 32], F32)
    Mt = sbt("Mt", [32, 32], F32)
    Xs = sbt("Xs", [32, 32], F32)
    Zs = sbt("Zs", [32, 32], F32)
    Xt_sb = sbt("Xt_sb", [32, 32], F32)
    Xt_bf = sbt("Xt_bf", [32, 32], BF16)
    attnT = sbt("attnT", [32, 32], BF16)
    rhs_sb = sbt("rhs_sb", [32, 512], BF16)
    u_f = sbt("u_f", [32, 256], F32)
    w_bf = sbt("w_bf", [32, 256], BF16)
    wT = sbt("wT", [128, 64], BF16)
    uadj_b = sbt("uadj_b", [32, 256], BF16)
    kn_b = sbt("kn_b", [32, 256], BF16)
    osb = sbt("osb", [32, 256], F32)
    S_f0 = sbt("S_f0", [128, 256], F32)
    S_f1 = sbt("S_f1", [128, 256], F32)
    S_b0 = sbt("S_b0", [128, 256], BF16)
    S_b1 = sbt("S_b1", [128, 256], BF16)
    pKK = ps("pKK", [32, 32])
    pKQ = ps("pKQ", [32, 32])
    pNw = ps("pNw", [32, 32])
    pBc = ps("pBc", [32, 32])
    pBig = ps("pBig", [32, 512])
    pS0 = ps("pS0", [128, 256])
    pS1 = ps("pS1", [128, 256])
    pTb = stack.enter_context(nc.psum_tensor(px + "pTb", [128, 64], BF16))

    def f0(e):
        pr.dma(e, smsl[:], msl)
        pr.dma(e, smsu[:], msu)
        pr.dma(e, smud[:], mud)
        pr.dma(e, sid32[:], id32)
        pr.dma(e, sidf[:], idf)
        pr.dma(e, sid[:], idb)
        pr.dma(e, sones[:], onesf)
        pr.dma(e, sixf[:], tqf)
        pr.dma(e, sixt[:], tqt)
    pr.on('sync', f0)

    def f0b(e):
        pr.gate(e)
        pr.t(e.tensor_scalar_mul(s2i[:], sid32[:], 2.0))
    pr.on('vector', f0b)

    for i in range(2):
        def fz(e):
            pr.gatec(e)
            pr.t(e.memset(S_f0[:], 0.0))
            pr.t(e.memset(S_f1[:], 0.0))
            pr.t(e.memset(S_b0[:], 0.0))
            pr.t(e.memset(S_b1[:], 0.0))
        pr.on('vector', fz)

        for c in range(NCH):
            l = c * 32
            half, lr = l // 4096, l % 4096

            def fl(e, i=i, c=c, half=half, lr=lr):
                pr.gatec(e)
                pr.gates(e)
                e.indirect_dma_start(qtc[:], None, qt_g,
                                     IOA(sixt[:, c:c + 1], 0), element_offset=i * 256).then_inc(pr.ld, 16)
                pr.l += 16
                e.indirect_dma_start(ktc[:], None, kt_g,
                                     IOA(sixt[:, c:c + 1], 0), element_offset=i * 256).then_inc(pr.ld, 16)
                pr.l += 16
                e.indirect_dma_start(vtc[:], None, vt_g,
                                     IOA(sixt[:, c:c + 1], 0), element_offset=i * 256).then_inc(pr.ld, 16)
                pr.l += 16
                e.indirect_dma_start(betac[:, 0:2], None, bt_g,
                                     IOA(sixt[:, c:c + 1], 0)).then_inc(pr.ld, 16)
                pr.l += 16
                for dc in range(2):
                    col = half * 4 + i * 2 + dc
                    e.indirect_dma_start(qfs[:, dc * 32:(dc + 1) * 32], None, qf_g,
                                         IOA(sixf[:, col:col + 1], 0),
                                         element_offset=128 + lr).then_inc(pr.ld, 16)
                    pr.l += 16
                    e.indirect_dma_start(kfs[:, dc * 32:(dc + 1) * 32], None, kf_g,
                                         IOA(sixf[:, col:col + 1], 0),
                                         element_offset=128 + lr).then_inc(pr.ld, 16)
                    pr.l += 16
            pr.on('gpsimd', fl)

            def n1(e):
                pr.gate(e)
                pr.t(e.activation(sq2[:], qtc[:], AF.Square))
            pr.on('scalar', n1)

            def n2(e):
                pr.gatec(e)
                pr.t(e.reduce_sum(ncol[:, 0:1], sq2[:], AX.X))
            pr.on('vector', n2)

            def n3(e):
                pr.gatec(e)
                pr.t(e.activation(sq2[:], ktc[:], AF.Square))
            pr.on('scalar', n3)

            def n4(e):
                pr.gatec(e)
                pr.t(e.reduce_sum(ncol[:, 1:2], sq2[:], AX.X))
            pr.on('vector', n4)

            def n5(e):
                pr.gatec(e)
                pr.t(e.activation(ncol[:, 2:3], ncol[:, 0:1], AF.Ln, bias=kq[1e-6][0:32, :]))
                pr.gatec(e)
                pr.t(e.activation(ncol[:, 2:3], ncol[:, 2:3], AF.Exp, scale=-0.5))
                pr.gatec(e)
                pr.t(e.activation(ncol[:, 3:4], ncol[:, 1:2], AF.Ln, bias=kq[1e-6][0:32, :]))
                pr.gatec(e)
                pr.t(e.activation(ncol[:, 3:4], ncol[:, 3:4], AF.Exp, scale=-0.5))
            pr.on('scalar', n5)

            def n6(e, i=i):
                pr.gatec(e)
                pr.t(e.tensor_mul(ncol[:, 4:5], betac[:, i:i + 1], ncol[:, 3:4]))
            pr.on('vector', n6)

            for (ci, mask, dst) in ((3, smsl, mbc1), (4, smsu, mbc2)):
                def b1(e, ci=ci):
                    pr.gatec(e)
                    pr.t(e.transpose(pNw[0:1, 0:32], ncol[:, ci:ci + 1], sidf[0:32, 0:32]))
                pr.on('tensor', b1)

                def b2(e):
                    pr.gatec(e)
                    pr.t(e.copy(rowb[:], pNw[0:1, 0:32]))
                pr.on('scalar', b2)

                def b3(e):
                    pr.gatec(e)
                    pr.t(e.matmul(pBc[:, 0:32], sones[0:1, 0:32], rowb[:], start=True, stop=True))
                pr.on('tensor', b3)

                def b4(e, mask=mask, dst=dst):
                    pr.gatec(e)
                    pr.t(e.tensor_mul(dst[:], pBc[:, 0:32], mask[:]))
                pr.on('vector', b4)

            def g1(e):
                pr.gatec(e)
                e.matmul(pKK[:], kfs[:, 0:32], kfs[:, 0:32], start=True, stop=False)
                pr.t(e.matmul(pKK[:], kfs[:, 32:64], kfs[:, 32:64], start=False, stop=True))
                e.matmul(pKQ[:], kfs[:, 0:32], qfs[:, 0:32], start=True, stop=False)
                pr.t(e.matmul(pKQ[:], kfs[:, 32:64], qfs[:, 32:64], start=False, stop=True))
            pr.on('tensor', g1)

            def g2(e):
                pr.gatec(e)
                pr.t(e.scalar_tensor_tensor(Nsb[:], pKK[:], ncol[:, 4:5], mbc1[:], ALU.mult, ALU.mult))
                pr.gatec(e)
                pr.t(e.scalar_tensor_tensor(Mt[:], pKK[:], ncol[:, 3:4], mbc2[:], ALU.mult, ALU.mult))
                pr.gatec(e)
                pr.t(e.scalar_tensor_tensor(attnT[:], pKQ[:], ncol[:, 3:4], smud[:], ALU.mult, ALU.mult))
            pr.on('vector', g2)

            def g3(e):
                pr.gatec(e)
                pr.t(e.scalar_tensor_tensor(Xs[:], Nsb[:], -1.0, sid32[:], ALU.mult, ALU.add))
                pr.gatec(e)
                pr.t(e.tensor_add(Mt[:], Mt[:], sid32[:]))
            pr.on('vector', g3)

            for it in range(4):
                def w1(e):
                    pr.gatec(e)
                    pr.t(e.matmul(pNw[:], Mt[:], Xs[:], start=True, stop=True))
                pr.on('tensor', w1)

                def w2(e):
                    pr.gatec(e)
                    pr.t(e.scalar_tensor_tensor(Zs[:], pNw[:], -1.0, s2i[:], ALU.mult, ALU.add))
                pr.on('vector', w2)

                def w3(e):
                    pr.gatec(e)
                    pr.t(e.transpose(pNw[:], Xs[:], sidf[0:32, 0:32]))
                pr.on('tensor', w3)

                def w4(e):
                    pr.gatec(e)
                    pr.t(e.copy(Xt_sb[:], pNw[:]))
                pr.on('scalar', w4)

                def w5(e):
                    pr.gatec(e)
                    pr.t(e.matmul(pNw[:], Xt_sb[:], Zs[:], start=True, stop=True))
                pr.on('tensor', w5)

                def w6(e):
                    pr.gatec(e)
                    pr.t(e.copy(Xs[:], pNw[:]))
                pr.on('scalar', w6)

            def a1(e):
                pr.gatec(e)
                pr.t(e.transpose(pNw[:], Xs[:], sidf[0:32, 0:32]))
            pr.on('tensor', a1)

            def a2(e):
                pr.gatec(e)
                pr.t(e.copy(Xt_bf[:], pNw[:]))
            pr.on('scalar', a2)

            def a3(e, i=i):
                pr.gatec(e)
                pr.t(e.tensor_scalar(rhs_sb[:, 0:256], vtc[:], betac[:, i:i + 1], None, ALU.mult))
                pr.gatec(e)
                pr.t(e.tensor_scalar(rhs_sb[:, 256:512], ktc[:], ncol[:, 4:5], None, ALU.mult))
                pr.gatec(e)
                pr.t(e.tensor_scalar(kn_b[:], ktc[:], ncol[:, 3:4], None, ALU.mult))
            pr.on('vector', a3)

            def a4(e):
                pr.gatec(e)
                pr.t(e.matmul(pBig[:, 0:512], Xt_bf[:], rhs_sb[:], start=True, stop=True))
            pr.on('tensor', a4)

            def a5(e):
                pr.gatec(e)
                pr.t(e.copy(u_f[:], pBig[:, 0:256]))
                pr.gatec(e)
                pr.t(e.copy(w_bf[:], pBig[:, 256:512]))
            pr.on('scalar', a5)

            def a6(e):
                pr.gatec(e)
                pr.t(e.transpose(pTb[:, 0:32], w_bf[:, 0:128], sid[0:32, 0:32]))
                pr.gatec(e)
                pr.t(e.transpose(pTb[:, 32:64], w_bf[:, 128:256], sid[0:32, 0:32]))
            pr.on('tensor', a6)

            def a7(e):
                pr.gatec(e)
                pr.t(e.copy(wT[:], pTb[:]))
            pr.on('scalar', a7)

            def a8(e):
                pr.gatec(e)
                e.matmul(pBig[:, 0:256], wT[:, 0:32], S_b0[:], start=True, stop=False)
                pr.t(e.matmul(pBig[:, 0:256], wT[:, 32:64], S_b1[:], start=False, stop=True))
            pr.on('tensor', a8)

            def a9(e):
                pr.gatec(e)
                pr.t(e.tensor_sub(u_f[:], u_f[:], pBig[:, 0:256]))
            pr.on('vector', a9)

            def a10(e):
                pr.gatec(e)
                pr.t(e.copy(uadj_b[:], u_f[:]))
            pr.on('scalar', a10)

            def a11(e):
                pr.gatec(e)
                e.matmul(pBig[:, 0:256], qfs[:, 0:32], S_b0[:], start=True, stop=False)
                e.matmul(pBig[:, 0:256], qfs[:, 32:64], S_b1[:], start=False, stop=False)
                pr.t(e.matmul(pBig[:, 0:256], attnT[:], uadj_b[:], start=False, stop=True))
            pr.on('tensor', a11)

            def a12(e):
                pr.gatec(e)
                pr.t(e.tensor_scalar(osb[:], pBig[:, 0:256], ncol[:, 2:3], None, ALU.mult))
            pr.on('vector', a12)

            def a13(e, i=i, half=half, lr=lr):
                pr.gatec(e)
                pr.sto(e, dout[half * 4096 + lr: half * 4096 + lr + 32, i * 256:(i + 1) * 256], osb[:])
            pr.on('gpsimd', a13)

            def a14(e):
                pr.gatec(e)
                e.matmul(pS0[:], kn_b[:, 0:128], uadj_b[:], start=True, stop=True)
                pr.t(e.matmul(pS1[:], kn_b[:, 128:256], uadj_b[:], start=True, stop=True))
            pr.on('tensor', a14)

            def a15(e):
                pr.gatec(e)
                pr.t(e.tensor_add(S_f0[:], S_f0[:], pS0[:]))
                pr.gatec(e)
                pr.t(e.tensor_add(S_f1[:], S_f1[:], pS1[:]))
            pr.on('vector', a15)

            def a16(e):
                pr.gatec(e)
                pr.t(e.copy(S_b0[:], S_f0[:]))
                pr.gatec(e)
                pr.t(e.copy(S_b1[:], S_f1[:]))
            pr.on('scalar', a16)


def emit_b3(pr, nc, stack, px, io):
    xin = io['x']
    vf, vt = io['vf'], io['vtl']
    dout_g = io['dout_g']
    tdout = io['tdout']
    fw, gw1, gb1, gw2, gb2, nw, wo = io['fw'], io['gw1'], io['gb1'], io['gw2'], io['gb2'], io['nw'], io['wo']
    idb, idf, onesf = io['idb'], io['idf'], io['ones']
    out = io['out']
    I32 = mybir.dt.int32
    sbt = lambda n, shp, dt_: stack.enter_context(nc.sbuf_tensor(px + n, shp, dt_))
    ps = lambda n, shp: stack.enter_context(nc.psum_tensor(px + n, shp, F32))
    kq = mkconsts(pr, stack, nc, [1e-5, 1e-6, 0.02], px + "kq")
    xt4 = sbt("xt4", [128, 4 * 1024], F32)
    xlt = sbt("xlt", [128, 1024], BF16)
    sq = sbt("sq", [128, 1024], F32)
    cols = sbt("cols", [128, 16], F32)
    sid = sbt("sid", [128, 128], BF16)
    sidf = sbt("sidf", [128, 128], F32)
    sones = sbt("sones", [1, 128], F32)
    fwsb = sbt("fwsb", [128, 8 * 57], F32)
    gb1s = sbt("gb1s", [128, 16], F32)
    gb2s = sbt("gb2s", [1, 28], F32)
    b2bc = sbt("b2bc", [128, 28], F32)
    nwr = sbt("nwr", [1, 1024], F32)
    nwb = sbt("nwb", [128, 1024], F32)
    gif = sbt("gif", [128, 9 * 512], BF16)
    strip = sbt("strip", [128, 544], BF16)
    acc = sbt("acc", [128, 512], F32)
    fbf = sbt("fbf", [128, 512], BF16)
    firt = sbt("firt", [128, 5 * 4 * 1024], BF16)
    delt = sbt("delt", [128, 4 * 1024], F32)
    v4t = sbt("v4t", [128, 4 * 1024], BF16)
    stats = sbt("stats", [128, 4 * 28], F32)
    statb = sbt("statb", [128, 28], BF16)
    gw1s = sbt("gw1s", [128, 8 * 128], BF16)
    gw1c = sbt("gw1c", [28, 128], BF16)
    gw2s = sbt("gw2s", [128, 16 * 28], BF16)
    hmid = sbt("hmid", [128, 16 * 512], BF16)
    wts = sbt("wts", [128, 4 * 28], F32)
    spt = sbt("spt", [128, 28 * 3], F32)
    ot = sbt("ot", [128, 1024], F32)
    ob = sbt("ob", [128, 1024], BF16)
    off_ = sbt("off", [128, 8 * 512], BF16)
    wosb = sbt("wosb", [128, 8 * 128], BF16)
    tmp = sbt("tmp", [128, 512], F32)
    otok4 = sbt("otok4", [128, 4 * 1024], F32)
    sidx = sbt("sidx", [128, 64], I32)
    pA = ps("pA", [128, 512])
    pL = ps("pL", [128, 28])
    pTf = ps("pTf", [128, 128])
    pT = stack.enter_context(nc.psum_tensor(px + "pT", [128, 128], BF16))

    def f0(e):
        pr.dma(e, sid[:], idb)
        pr.dma(e, sidf[:], idf)
        pr.dma(e, sones[:], onesf)
        pr.dma(e, fwsb[:], fw)
        pr.dma(e, gb1s[:], gb1)
        pr.dma(e, gb2s[:], gb2)
        pr.dma(e, nwr[:], nw)
        pr.dma(e, sidx[:], tdout)
        pr.dma(e, gw2s[:].rearrange("p (kc j) -> p kc j", j=28),
               gw2.rearrange("(kc p) j -> p kc j", p=128))
    pr.on('sync', f0)

    def fb(e):
        pr.gate(e)
        pr.t(e.matmul(pL[:, 0:28], sones[:], gb2s[:], start=True, stop=True))
    pr.on('tensor', fb)

    def fb2(e):
        pr.gatec(e)
        pr.t(e.copy(b2bc[:], pL[:, 0:28]))
    pr.on('scalar', fb2)
    for q in range(2):
        def fb3(e, q=q):
            pr.gatec(e)
            pr.t(e.matmul(pA[:, 0:512], sones[:], nwr[:, q * 512:(q + 1) * 512], start=True, stop=True))
        pr.on('tensor', fb3)

        def fb4(e, q=q):
            pr.gatec(e)
            pr.t(e.copy(nwb[:, q * 512:(q + 1) * 512], pA[:, 0:512]))
        pr.on('scalar', fb4)

    for blk in range(TPC // 512):
        t0h = 128 + blk * 512
        for r4 in range(4):
            def fl(e, r4=r4, blk=blk, t0h=t0h):
                pr.gatec(e)
                pr.gates(e)
                pr.dma(e, xt4[:, r4 * 1024:(r4 + 1) * 1024],
                       xin[t0h + r4 * 128: t0h + (r4 + 1) * 128, :])
                pr.dma(e, v4t[:, r4 * 1024:(r4 + 1) * 1024],
                       vt[t0h + r4 * 128: t0h + (r4 + 1) * 128, :])
            pr.on('sync', fl)

            def fld(e, r4=r4, blk=blk):
                pr.gatec(e)
                for h in range(4):
                    col = (h // 2) * 32 + blk * 4 + r4
                    e.indirect_dma_start(delt[:, r4 * 1024 + h * 256: r4 * 1024 + (h + 1) * 256],
                                         None, dout_g, IOA(sidx[:, col:col + 1], 0),
                                         element_offset=(h % 2) * 256).then_inc(pr.ld, 16)
                    pr.l += 16
            pr.on('gpsimd', fld)
            ln_tile(pr, xt4[:, r4 * 1024:(r4 + 1) * 1024], xlt[:],
                    {'sq': sq, 'cols': cols, 'eps': kq[1e-5]})
            for kc2 in range(8):
                def ftr(e, kc2=kc2):
                    pr.gate(e)
                    pr.t(e.transpose(pT[:, 0:128], xlt[:, kc2 * 128:(kc2 + 1) * 128], sid[:]))
                pr.on('tensor', ftr)

                def fcp(e, kc2=kc2, r4=r4):
                    pr.gatec(e)
                    pr.t(e.copy(gif[:, kc2 * 512 + r4 * 128: kc2 * 512 + (r4 + 1) * 128], pT[:, 0:128]))
                pr.on('scalar', fcp)

        for cb in range(8):
            def fls(e, cb=cb, t0h=t0h):
                pr.gatec(e)
                pr.dma(e, strip[:], vf[cb * 128:(cb + 1) * 128, t0h - 32: t0h + 512])
            pr.on('sync', fls)
            for s_, K in enumerate(FIRK):
                def fcv(e, s_=s_, K=K, cb=cb):
                    pr.gate(e)
                    base = cb * 57 + FIROFF[s_]
                    pr.t(e.tensor_scalar(acc[:], strip[:, 32 - (K - 1): 32 - (K - 1) + 512],
                                         fwsb[:, base:base + 1], None, ALU.mult))
                    for j in range(1, K):
                        pr.gatec(e)
                        pr.t(e.scalar_tensor_tensor(acc[:], strip[:, 32 - (K - 1) + j: 32 - (K - 1) + j + 512],
                                                    fwsb[:, base + j:base + j + 1],
                                                    acc[:], ALU.mult, ALU.add))
                pr.on('vector', fcv)

                def fcb(e):
                    pr.gatec(e)
                    pr.t(e.copy(fbf[:], acc[:]))
                pr.on('scalar', fcb)
                for q in range(4):
                    def ftr(e, q=q):
                        pr.gate(e)
                        pr.t(e.transpose(pT[:, 0:128], fbf[:, q * 128:(q + 1) * 128], sid[:]))
                    pr.on('tensor', ftr)

                    def fcp(e, q=q, s_=s_, cb=cb):
                        pr.gatec(e)
                        pr.t(e.copy(firt[:, s_ * 4096 + q * 1024 + cb * 128: s_ * 4096 + q * 1024 + (cb + 1) * 128],
                                    pT[:, 0:128]))
                    pr.on('scalar', fcp)

        for r4 in range(4):
            def fst_(e, r4=r4):
                pr.gate(e)
                for s_ in range(7):
                    for h in range(4):
                        if s_ < 5:
                            src = firt[:, s_ * 4096 + r4 * 1024 + h * 256: s_ * 4096 + r4 * 1024 + (h + 1) * 256]
                        elif s_ == 5:
                            src = delt[:, r4 * 1024 + h * 256: r4 * 1024 + (h + 1) * 256]
                        else:
                            src = v4t[:, r4 * 1024 + h * 256: r4 * 1024 + (h + 1) * 256]
                        pr.t(e.reduce_sum(stats[:, r4 * 28 + s_ * 4 + h: r4 * 28 + s_ * 4 + h + 1],
                                          src, AX.X, apply_absolute_value=True))
                pr.gatec(e)
                pr.t(e.tensor_scalar_mul(stats[:, r4 * 28:(r4 + 1) * 28],
                                         stats[:, r4 * 28:(r4 + 1) * 28], 1.0 / 256))
            pr.on('vector', fst_)

            def fsb(e, r4=r4):
                pr.gatec(e)
                pr.t(e.copy(statb[:], stats[:, r4 * 28:(r4 + 1) * 28]))
            pr.on('scalar', fsb)

            def ftr(e, r4=r4):
                pr.gate(e)
                pr.t(e.transpose(pT[0:28, 0:128], statb[:], sid[:]))
            pr.on('tensor', ftr)

            def fcp(e, r4=r4):
                pr.gatec(e)
                pr.t(e.copy(gif[0:28, 8 * 512 + r4 * 128: 8 * 512 + (r4 + 1) * 128], pT[0:28, 0:128]))
            pr.on('scalar', fcp)

        for jo in range(16):
            def fw1(e, jo=jo):
                pr.gatec(e)
                pr.dma(e, gw1s[:].rearrange("p (kc j) -> p kc j", j=128),
                       gw1[0:1024, jo * 128:(jo + 1) * 128].rearrange("(kc p) j -> p kc j", p=128))
                pr.dma(e, gw1c[:], gw1[1024:1052, jo * 128:(jo + 1) * 128])
            pr.on('sync', fw1)

            def fmm(e, jo=jo):
                pr.gate(e)
                for ki in range(8):
                    e.matmul(pA[:, 0:512], gw1s[:, ki * 128:(ki + 1) * 128],
                             gif[:, ki * 512:(ki + 1) * 512], start=(ki == 0), stop=False)
                pr.t(e.matmul(pA[:, 0:512], gw1c[:], gif[0:28, 8 * 512:9 * 512],
                              start=False, stop=True))
            pr.on('tensor', fmm)

            def fev(e, jo=jo):
                pr.gatec(e)
                pr.t(e.activation(hmid[:, jo * 512:(jo + 1) * 512], pA[:, 0:512],
                                  AF.Gelu, bias=gb1s[:, jo:jo + 1]))
            pr.on('scalar', fev)

        for r4 in range(4):
            def fmm2(e, r4=r4):
                pr.gatec(e)
                for ki in range(16):
                    m = e.matmul(pL[:, 0:28], hmid[:, ki * 512 + r4 * 128: ki * 512 + (r4 + 1) * 128],
                                 gw2s[:, ki * 28:(ki + 1) * 28], start=(ki == 0), stop=(ki == 15))
                    if ki == 15:
                        pr.t(m)
            pr.on('tensor', fmm2)

            def fsp(e):
                pr.gatec(e)
                pr.t(e.tensor_add(spt[:, 0:28], pL[:, 0:28], b2bc[:]))
            pr.on('vector', fsp)

            def fsp2(e):
                pr.gatec(e)
                pr.t(e.activation(spt[:, 28:56], spt[:, 0:28], AF.Abs))
                pr.gatec(e)
                pr.t(e.activation(spt[:, 28:56], spt[:, 28:56], AF.Exp, scale=-1.0))
                pr.gatec(e)
                pr.t(e.activation(spt[:, 28:56], spt[:, 28:56], AF.Ln, bias=1.0))
            pr.on('scalar', fsp2)

            def fsp3(e, r4=r4):
                pr.gatec(e)
                pr.t(e.tensor_scalar_max(spt[:, 56:84], spt[:, 0:28], 0.0))
                pr.gatec(e)
                pr.t(e.tensor_add(spt[:, 56:84], spt[:, 28:56], spt[:, 56:84]))
                pr.gatec(e)
                pr.t(e.tensor_scalar_add(spt[:, 56:84], spt[:, 56:84], 0.02))
                for h in range(4):
                    pr.gatec(e)
                    pr.t(e.reduce_sum(cols[:, 8 + h:9 + h], spt[:, 56 + h * 7:56 + (h + 1) * 7], AX.X))
                    pr.gatec(e)
                    pr.t(e.reciprocal(cols[:, 12:13], cols[:, 8 + h:9 + h]))
                    pr.gatec(e)
                    pr.t(e.tensor_scalar(wts[:, r4 * 28 + h * 7: r4 * 28 + (h + 1) * 7],
                                         spt[:, 56 + h * 7:56 + (h + 1) * 7],
                                         cols[:, 12:13], None, ALU.mult))
            pr.on('vector', fsp3)

        for r4 in range(4):
            def fcmb(e, r4=r4):
                pr.gatec(e)
                for h in range(4):
                    pr.t(e.tensor_scalar(ot[:, h * 256:(h + 1) * 256],
                                         firt[:, r4 * 1024 + h * 256: r4 * 1024 + (h + 1) * 256],
                                         wts[:, r4 * 28 + h * 7: r4 * 28 + h * 7 + 1], None, ALU.mult))
                for s_ in range(1, 7):
                    pr.gatec(e)
                    for h in range(4):
                        if s_ < 5:
                            src = firt[:, s_ * 4096 + r4 * 1024 + h * 256: s_ * 4096 + r4 * 1024 + (h + 1) * 256]
                        elif s_ == 5:
                            src = delt[:, r4 * 1024 + h * 256: r4 * 1024 + (h + 1) * 256]
                        else:
                            src = v4t[:, r4 * 1024 + h * 256: r4 * 1024 + (h + 1) * 256]
                        pr.t(e.scalar_tensor_tensor(ot[:, h * 256:(h + 1) * 256], src,
                                                    wts[:, r4 * 28 + h * 7 + s_: r4 * 28 + h * 7 + s_ + 1],
                                                    ot[:, h * 256:(h + 1) * 256], ALU.mult, ALU.add))
            pr.on('vector', fcmb)

            def fsq(e):
                pr.gatec(e)
                pr.t(e.activation(sq[:], ot[:], AF.Square))
            pr.on('scalar', fsq)

            def fms(e):
                pr.gatec(e)
                for h in range(4):
                    pr.t(e.reduce_sum(cols[:, h:h + 1], sq[:, h * 256:(h + 1) * 256], AX.X))
                pr.gatec(e)
                pr.t(e.tensor_scalar_mul(cols[:, 0:4], cols[:, 0:4], 1.0 / 256))
            pr.on('vector', fms)

            def frs(e):
                pr.gatec(e)
                pr.t(e.activation(cols[:, 4:8], cols[:, 0:4], AF.Ln, bias=kq[1e-5]))
                pr.gatec(e)
                pr.t(e.activation(cols[:, 4:8], cols[:, 4:8], AF.Exp, scale=-0.5))
            pr.on('scalar', frs)

            def fnw(e, r4=r4):
                for h in range(4):
                    pr.gatec(e)
                    pr.t(e.tensor_scalar(ot[:, h * 256:(h + 1) * 256], ot[:, h * 256:(h + 1) * 256],
                                         cols[:, 4 + h:5 + h], None, ALU.mult))
                pr.gatec(e)
                pr.t(e.tensor_mul(ob[:], ot[:], nwb[:]))
            pr.on('vector', fnw)
            for kc2 in range(8):
                def ftr(e, kc2=kc2):
                    pr.gate(e)
                    pr.t(e.transpose(pT[:, 0:128], ob[:, kc2 * 128:(kc2 + 1) * 128], sid[:]))
                pr.on('tensor', ftr)

                def fcp(e, kc2=kc2, r4=r4):
                    pr.gatec(e)
                    pr.t(e.copy(off_[:, kc2 * 512 + r4 * 128: kc2 * 512 + (r4 + 1) * 128], pT[:, 0:128]))
                pr.on('scalar', fcp)

        for do in range(8):
            def fwo(e, do=do):
                pr.gatec(e)
                pr.dma(e, wosb[:].rearrange("p (kc j) -> p kc j", j=128),
                       wo[:, do * 128:(do + 1) * 128].rearrange("(kc p) j -> p kc j", p=128))
            pr.on('sync', fwo)

            def fmm3(e, do=do):
                pr.gate(e)
                for ki in range(8):
                    m = e.matmul(pA[:, 0:512], wosb[:, ki * 128:(ki + 1) * 128],
                                 off_[:, ki * 512:(ki + 1) * 512], start=(ki == 0), stop=(ki == 7))
                    if ki == 7:
                        pr.t(m)
            pr.on('tensor', fmm3)

            def fev2(e, do=do):
                pr.gatec(e)
                pr.t(e.copy(tmp[:], pA[:, 0:512]))
            pr.on('scalar', fev2)
            for q in range(4):
                def ftr(e, q=q):
                    pr.gate(e)
                    pr.t(e.transpose(pTf[:, 0:128], tmp[:, q * 128:(q + 1) * 128], sidf[:]))
                pr.on('tensor', ftr)

                def fadd(e, do=do, q=q):
                    pr.gatec(e)
                    pr.gates(e)
                    pr.t(e.tensor_add(otok4[:, q * 1024 + do * 128: q * 1024 + (do + 1) * 128],
                                      pTf[:, 0:128], xt4[:, q * 1024 + do * 128: q * 1024 + (do + 1) * 128]))
                pr.on('vector', fadd)

        for q in range(4):
            def fsto(e, q=q, blk=blk):
                pr.gatec(e)
                pr.sto(e, out[blk * 512 + q * 128: blk * 512 + (q + 1) * 128, :],
                       otok4[:, q * 1024:(q + 1) * 1024])
            pr.on('gpsimd', fsto)



def build_fused(part=-1, debug=False):
    nc = bass.Bass("TRN2", target_bir_lowering=False)
    stack = ExitStack()
    I32 = mybir.dt.int32
    ei = lambda n, shp, dt: nc.dram_tensor(n, shp, dt, kind="ExternalInput").ap()
    it = lambda n, shp, dt: nc.dram_tensor(n, shp, dt, kind="Internal")

    x_in = ei("x", [TA, D], F32)
    win_f = ei("wflat", [WPC, WC], F32)
    bg = ei("bg", [1, 8], F32)
    cw = ei("cw", [128, 68], F32)
    cbv = ei("cbv", [128, 17], F32)
    dtb = ei("dtb", [1, 32], F32)
    alog = ei("alog", [1, 16], F32)
    dvv = ei("dvv", [1, 16], F32)
    nwm = ei("nwm", [1, 2048], F32)
    scl = ei("scl", [128, 136], F32)
    cq = ei("cq", [128, 32], F32)
    ck = ei("ck", [128, 32], F32)
    cv = ei("cv", [128, 32], F32)
    fw = ei("fw", [128, 456], F32)
    gb1 = ei("gb1", [128, 16], F32)
    gb2 = ei("gb2", [1, 28], F32)
    nwd = ei("nwd", [1, 1024], F32)
    idb = ei("c_idb", [128, 128], BF16)
    idf = ei("c_idf", [128, 128], F32)
    onesf = ei("c_ones", [1, 128], F32)
    e8 = ei("c_e8", [8, 8 * 128], F32)
    triu = ei("c_triu", [64, 64], F32)
    mneg = ei("c_mneg", [64, 64], F32)
    sel = ei("c_sel", [16, 16 * 64], F32)
    msl = ei("c_msl", [32, 32], F32)
    msu = ei("c_msu", [32, 32], F32)
    mud = ei("c_mud", [32, 32], F32)
    id32 = ei("c_id32", [32, 32], F32)
    tA2 = ei("t_a2", [64, 128], I32)
    tA3 = ei("t_a3", [128, 66], I32)
    tqf = ei("t_qf", [128, 8], I32)
    tqt = ei("t_qt", [32, 256], I32)
    tdout = ei("t_dout", [128, 64], I32)
    yout = nc.dram_tensor("out", [TPC, D], F32, kind="ExternalOutput").ap()
    if debug:
        dbg = {n: nc.dram_tensor("dbg_" + n, shp, dt, kind="ExternalOutput").ap()
               for (n, shp, dt) in [("x1", [TA, D], F32), ("x2", [TA, D], F32),
                                    ("x3", [TA, D], F32), ("x4", [TA, D], F32),
                                    ("x5", [TPC, D], F32), ("y", [2 * TA, 1024], F32),
                                    ("do", [2 * 4096, 512], F32), ("z", [TA, 2048], F32)]}

    S8 = WSPLIT // 8
    wsh0 = it("wsh0", [S8, WC], BF16)
    wsh1 = it("wsh1", [WPC - S8, WC], BF16)
    wbf0 = it("wbf0", [WSPLIT, WC], BF16)
    wbf1 = it("wbf1", [WROWS - WSPLIT, WC], BF16)
    x1 = it("x1", [TA, D], F32)
    if part == 0:
        x2 = nc.dram_tensor("x2o", [TA, D], F32, kind="ExternalOutput")
    elif part == 1:
        x2 = nc.dram_tensor("x2i", [TA, D], F32, kind="ExternalInput")
    else:
        x2 = it("x2", [TA, D], F32)
    x3 = it("x3", [TA, D], F32)
    if part == 1:
        x4 = nc.dram_tensor("x4o", [TA, D], F32, kind="ExternalOutput")
    elif part == 2:
        x4 = nc.dram_tensor("x4i", [TA, D], F32, kind="ExternalInput")
    else:
        x4 = it("x4", [TA, D], F32)
    x5 = it("x5", [TPC, D], F32)
    z_l = it("z_l", [TA, 2048], F32)
    dtstk = it("dtstk", [2 * TA, 16], F32)
    dt_g = it("dt_g", [4 * TA, 16], F32)
    xhstk = it("xhstk", [2 * TA, 1024], BF16)
    xh_g = it("xh_g", [4 * TA, 1024], BF16)
    bct_l = it("bct_l", [TA, 32], BF16)
    bct_g = it("bct_g", [2 * TA, 32], BF16)
    bcf_l = it("bcf_l", [32, TA], BF16)
    bcf_g = it("bcf_g", [64, TA], BF16)
    ystk = it("ystk", [2 * TA, 1024], F32)
    y_g = it("y_g", [4 * TA, 1024], F32)
    qf_l = it("qf_l", [1024, TA], BF16)
    kf_l = it("kf_l", [1024, TA], BF16)
    vf_l = it("vf_l", [1024, TA], BF16)
    qf_g = it("qf_g", [2048, TA], BF16)
    kf_g = it("kf_g", [2048, TA], BF16)
    qtstk = it("qtstk", [2 * TA, 512], BF16)
    ktstk = it("ktstk", [2 * TA, 512], BF16)
    vtstk = it("vtstk", [2 * TA, 512], BF16)
    qt_g = it("qt_g", [4 * TA, 512], BF16)
    kt_g = it("kt_g", [4 * TA, 512], BF16)
    vt_g = it("vt_g", [4 * TA, 512], BF16)
    vt_l = it("vt_l", [TA, 1024], BF16)
    btstk = it("btstk", [2 * TA, 2], F32)
    bt_g = it("bt_g", [4 * TA, 2], F32)
    doutstk = it("doutstk", [2 * 4096, 512], F32)
    dout_g = it("dout_g", [4 * 4096, 512], F32)

    halvesA = [(0, 128), (128, 2048), (2176, 2048)]
    chunksA = [(g * 512, 512) for g in range(8)] + [(4096, 128)]
    chunksP = [(g * 512, 512) for g in range(8)]
    groupsA = [(g * 512, 512) for g in range(8)] + [(4096, 128)]

    with nc.Block() as block:
        pr = P2(nc, block, stack)
        st0 = ExitStack()
        emit_wcast2(pr, nc, st0, "w_", win_f, wsh0.ap(), wsh1.ap(), S8)
        st0.close()
        pr.stage()
        emit_coll(pr, [(wsh0, wbf0), (wsh1, wbf1)], rg=RG_ALL)
        pr.stage()

        wv = lambda n: wap(wbf0.ap(), wbf1.ap(), n)
        stm = ExitStack()
        part in (-1, 0) and emit_moe(pr, nc, stm, "m1_", dict(
            x=x_in, out=x1.ap(), wgt=wv('l0_wgT'), wgate=wv('l0_wgateT'),
            wup=wv('l0_wupT'), wdown=wv('l0_wdownT'), bg=bg, idb=idb, idf=idf,
            e8=e8, ones=onesf, halves=halvesA))
        stm.close()
        pr.stage()

        sta = ExitStack()
        part in (-1, 0) and emit_a1(pr, nc, sta, "a1_", dict(
            x=x1.ap(), win=wv('l0_winT'), cw=cw, cb=cbv, dtb=dtb, idb=idb, ones=onesf,
            z=z_l.ap(), dtstk=dtstk.ap(), xhstk=xhstk.ap(), bct=bct_l.ap(), bcf=bcf_l.ap()))
        sta.close()
        pr.stage()
        part in (-1, 0) and emit_coll(pr, [(xhstk, xh_g), (dtstk, dt_g), (bct_l, bct_g), (bcf_l, bcf_g)])
        pr.stage()

        st2 = ExitStack()
        part in (-1, 0) and emit_a2(pr, nc, st2, "a2_", dict(
            xh_g=xh_g.ap(), bct_g=bct_g.ap(), bcf_g=bcf_g.ap(), dt_g=dt_g.ap(), tA2=tA2,
            alog=alog, dv=dvv, triu=triu, mneg=mneg, sel=sel, ones=onesf, idb=idb, idf=idf,
            ystk=ystk.ap()))
        st2.close()
        pr.stage()
        part in (-1, 0) and emit_coll(pr, [(ystk, y_g)])
        pr.stage()

        st3 = ExitStack()
        part in (-1, 0) and emit_a3(pr, nc, st3, "a3_", dict(
            y_g=y_g.ap(), tA3=tA3, z=z_l.ap(), xr=x1.ap(), out=x2.ap(),
            wo=wv('l0_woutT'), nw=nwm, idb=idb, idf=idf, ones=onesf, groups=groupsA))
        st3.close()
        pr.stage()

        stm2 = ExitStack()
        part in (-1, 1) and emit_moe(pr, nc, stm2, "m2_", dict(
            x=x2.ap(), out=x3.ap(), wgt=wv('l0_wgT'), wgate=wv('l0_wgateT'),
            wup=wv('l0_wupT'), wdown=wv('l0_wdownT'), bg=bg, idb=idb, idf=idf,
            e8=e8, ones=onesf, halves=halvesA))
        stm2.close()
        pr.stage()

        stt = ExitStack()
        part in (-1, 1) and emit_tffn(pr, nc, stt, "t1_", dict(
            x=x3.ap(), out=x4.ap(), w1=wv('l1_w1T'), w2=wv('l1_w2T'), h1=wv('l1_h1T'),
            h2=wv('l1_h2T'), w3=wv('l1_w3T'), scl=scl, idb=idb, idf=idf, chunks=chunksA))
        stt.close()
        pr.stage()

        stb = ExitStack()
        part in (-1, 2) and emit_b1(pr, nc, stb, "b1_", dict(
            x=x4.ap(), wq=wv('l1_wq'), wk=wv('l1_wk'), wv=wv('l1_wv'), wbv=wv('l1_wb'),
            cq=cq, ck=ck, cv=cv, idb=idb,
            qf=qf_l.ap(), kf=kf_l.ap(), vf=vf_l.ap(), qtstk=qtstk.ap(), ktstk=ktstk.ap(),
            vtstk=vtstk.ap(), vtl=vt_l.ap(), btstk=btstk.ap()))
        stb.close()
        pr.stage()
        part in (-1, 2) and emit_coll(pr, [(qf_l, qf_g), (kf_l, kf_g), (qtstk, qt_g), (ktstk, kt_g),
                       (vtstk, vt_g), (btstk, bt_g)])
        pr.stage()

        stb2 = ExitStack()
        part in (-1, 2) and emit_b2(pr, nc, stb2, "b2_", dict(
            qf_g=qf_g.ap(), kf_g=kf_g.ap(), qt_g=qt_g.ap(), kt_g=kt_g.ap(), vt_g=vt_g.ap(),
            bt_g=bt_g.ap(), tqf=tqf, tqt=tqt, msl=msl, msu=msu, mud=mud, id32=id32,
            idf=idf, idb=idb, ones=onesf, doutstk=doutstk.ap()))
        stb2.close()
        pr.stage()
        part in (-1, 2) and emit_coll(pr, [(doutstk, dout_g)])
        pr.stage()

        stb3 = ExitStack()
        part in (-1, 2) and emit_b3(pr, nc, stb3, "b3_", dict(
            x=x4.ap(), vf=vf_l.ap(), vtl=vt_l.ap(), dout_g=dout_g.ap(), tdout=tdout,
            fw=fw, gw1=wv('l1_gw1'), gb1=gb1, gw2=wv('l1_gw2'), gb2=gb2, nw=nwd,
            wo=wv('l1_wo'), idb=idb, idf=idf, ones=onesf, out=x5.ap()))
        stb3.close()
        pr.stage()

        stt2 = ExitStack()
        part in (-1, 2) and emit_tffn(pr, nc, stt2, "t2_", dict(
            x=x5.ap(), out=yout, w1=wv('l1_w1T'), w2=wv('l1_w2T'), h1=wv('l1_h1T'),
            h2=wv('l1_h2T'), w3=wv('l1_w3T'), scl=scl, idb=idb, idf=idf, chunks=chunksP))
        stt2.close()

        if debug:
            pr.stage()

            def fdbg(e):
                pr.gatec(e)
                pr.gates(e)
                pr.sto(e, dbg["x1"], x1.ap()[:, :])
                pr.sto(e, dbg["x2"], x2.ap()[:, :])
                pr.sto(e, dbg["x3"], x3.ap()[:, :])
                pr.sto(e, dbg["x4"], x4.ap()[:, :])
                pr.sto(e, dbg["x5"], x5.ap()[:, :])
                pr.sto(e, dbg["y"], ystk.ap()[:, :])
                pr.sto(e, dbg["do"], doutstk.ap()[:, :])
                pr.sto(e, dbg["z"], z_l.ap()[:, :])
            pr.on('gpsimd', fdbg)

        def fin(e):
            e.wait_ge(pr.sm, pr.n)
            for (sem, v) in pr.prev:
                if v:
                    e.wait_ge(sem, v)
        pr.on('gpsimd', fin)
        pr.flush()
    stack.close()
    return nc


_fused_nc = None
_fused_ncs = None


def kernel(debug=False, **inp):
    global _fused_nc
    a = lambda n: np.ascontiguousarray(np.asarray(inp[n]))
    wdict = {
        'l0_wgT': np.ascontiguousarray(a('L0_ffn_wg').T),
        'l0_wgateT': np.ascontiguousarray(np.transpose(a('L0_ffn_Wgate'), (0, 2, 1))),
        'l0_wupT': np.ascontiguousarray(np.transpose(a('L0_ffn_Wup'), (0, 2, 1))),
        'l0_wdownT': np.ascontiguousarray(np.transpose(a('L0_ffn_Wdown'), (0, 2, 1))),
        'l0_winT': np.ascontiguousarray(a('L0_attn_W_in').T),
        'l0_woutT': np.ascontiguousarray(a('L0_attn_W_out').T),
        'l1_w1T': np.ascontiguousarray(a('L1_ffn_w1_w').T),
        'l1_w2T': np.ascontiguousarray(a('L1_ffn_w2_w').T),
        'l1_h1T': np.ascontiguousarray(a('L1_ffn_h1_w').T),
        'l1_h2T': np.ascontiguousarray(a('L1_ffn_h2_w').T),
        'l1_w3T': np.ascontiguousarray(a('L1_ffn_w3_w').T),
        'l1_wq': a('L1_attn_w_q'), 'l1_wk': a('L1_attn_w_k'), 'l1_wv': a('L1_attn_w_v'),
        'l1_wb': a('L1_attn_w_b'), 'l1_gw1': a('L1_attn_gate_w1'),
        'l1_gw2': a('L1_attn_gate_w2'), 'l1_wo': a('L1_attn_w_o'),
    }
    big = pack_weights(wdict)

    sc = lambda v: np.ascontiguousarray(np.asarray(v).reshape(-1, 128).T)
    sclv = np.concatenate([sc(inp['L1_ffn_w1_s']), sc(inp['L1_ffn_w2_s']),
                           sc(inp['L1_ffn_h1_s']), sc(inp['L1_ffn_h2_s']),
                           sc(inp['L1_ffn_w3_s'])], axis=1).astype(np.float32)
    cwv = np.zeros((2176, 4), np.float32)
    cwv[:2080] = a('L0_attn_conv_w')
    cwv = np.ascontiguousarray(cwv.reshape(17, 128, 4).transpose(1, 0, 2).reshape(128, 68))
    cbvv = np.zeros((2176,), np.float32)
    cbvv[:2080] = a('L0_attn_conv_b')
    cbvv = np.ascontiguousarray(cbvv.reshape(17, 128).T)
    ctile = lambda w: np.ascontiguousarray(w.reshape(8, 128, w.shape[1]).transpose(1, 0, 2).reshape(128, -1))
    firw = np.zeros((1024, 57), np.float32)
    col = 0
    for nm, K in (("fir1", 1), ("fir3", 3), ("fir7", 7), ("fir15", 15), ("fir31", 31)):
        firw[:, col:col + K] = a('L1_attn_' + nm)
        col += K
    fwv = np.ascontiguousarray(firw.reshape(8, 128, 57).transpose(1, 0, 2).reshape(128, 456))
    gb1v = np.ascontiguousarray(a('L1_attn_gate_b1').reshape(16, 128).T)
    gb2v = a('L1_attn_gate_b2').reshape(1, 28).astype(np.float32)
    nwdv = np.tile(a('L1_attn_norm_w'), 4).reshape(1, 1024).astype(np.float32)
    nwmv = a('L0_attn_norm_w').reshape(1, 2048).astype(np.float32)
    dtbv = a('L0_attn_dt_bias').reshape(1, 32).astype(np.float32)
    alogv = a('L0_attn_A_log').astype(np.float32)
    dvvv = a('L0_attn_D').astype(np.float32)

    idbv = np.eye(128, dtype=BF)
    idfv = np.eye(128, dtype=np.float32)
    onesv = np.ones((1, 128), np.float32)
    e8v = np.zeros((8, 8 * 128), np.float32)
    for ex in range(8):
        e8v[ex, ex * 128:(ex + 1) * 128] = 1.0
    triuv = np.triu(np.ones((64, 64), np.float32))
    mnegv = np.where(np.arange(64)[None, :] >= np.arange(64)[:, None], 0.0, -1e9).astype(np.float32)
    selv = np.zeros((16, 16 * 64), np.float32)
    for h in range(16):
        selv[h, h * 64:(h + 1) * 64] = 1.0
    mslv = np.tril(np.ones((32, 32), np.float32), -1)
    msuv = np.triu(np.ones((32, 32), np.float32), 1)
    mudv = np.triu(np.ones((32, 32), np.float32))
    id32v = np.eye(32, dtype=np.float32)

    x = np.ascontiguousarray(np.asarray(inp['L0_attn_x']).reshape(TT, D))
    ar = np.arange

    global _fused_ncs
    if _fused_ncs is None:
        _fused_ncs = [build_fused(part=p) for p in (0, 1, 2)]

    im = []
    for c in range(8):
        t0 = c * TPC
        xa = np.zeros((TA, D), np.float32)
        xa[128:] = x[t0:t0 + TPC]
        if c % 2 == 1:
            xa[:128] = x[t0 - 128:t0]
        role = c % 2
        t_a2 = np.zeros((64, 128), np.int32)
        for k in range(128):
            l = k * 64
            t_a2[:, k] = (l // 4096) * 2 * TA + role * TA + 128 + (l % 4096) + ar(64)
        t_a3 = np.zeros((128, 66), np.int32)
        for r in range(33):
            t_a3[:, 2 * r] = role * TA + r * 128 + ar(128)
            t_a3[:, 2 * r + 1] = 2 * TA + role * TA + r * 128 + ar(128)
        t_qf = np.zeros((128, 8), np.int32)
        for half in range(2):
            for i in range(2):
                for dc in range(2):
                    t_qf[:, half * 4 + i * 2 + dc] = half * 1024 + role * 512 + i * 256 + dc * 128 + ar(128)
        t_qt = np.zeros((32, 256), np.int32)
        for cc in range(256):
            l = cc * 32
            t_qt[:, cc] = ((l // 4096) * 2 + role) * TA + 128 + (l % 4096) + ar(32)
        t_do = np.zeros((128, 64), np.int32)
        for hh in range(2):
            for blk in range(8):
                for r4 in range(4):
                    t_do[:, hh * 32 + blk * 4 + r4] = (hh * 2 + role) * 4096 + blk * 512 + r4 * 128 + ar(128)

        im.append({
            "x": xa, "wflat": np.ascontiguousarray(np.concatenate(
                [big[c * (WSPLIT // 8):(c + 1) * (WSPLIT // 8)],
                 big[WSPLIT + c * ((WROWS - WSPLIT) // 8): WSPLIT + (c + 1) * ((WROWS - WSPLIT) // 8)]], axis=0)),
            "bg": a('L0_ffn_bg').reshape(1, 8), "cw": cwv, "cbv": cbvv, "dtb": dtbv,
            "alog": np.ascontiguousarray(alogv[role * 16:(role + 1) * 16].reshape(1, 16)),
            "dvv": np.ascontiguousarray(dvvv[role * 16:(role + 1) * 16].reshape(1, 16)),
            "nwm": nwmv, "scl": sclv, "cq": ctile(a('L1_attn_conv_q_w')),
            "ck": ctile(a('L1_attn_conv_k_w')), "cv": ctile(a('L1_attn_conv_v_w')),
            "fw": fwv, "gb1": gb1v, "gb2": gb2v, "nwd": nwdv,
            "c_idb": idbv, "c_idf": idfv, "c_ones": onesv, "c_e8": e8v,
            "c_triu": triuv, "c_mneg": mnegv, "c_sel": selv, "c_msl": mslv,
            "c_msu": msuv, "c_mud": mudv, "c_id32": id32v,
            "t_a2": t_a2, "t_a3": t_a3, "t_qf": t_qf, "t_qt": t_qt, "t_dout": t_do,
        })
    r0 = run_bass_kernel_spmd(_fused_ncs[0], im, core_ids=list(range(8)))
    im1 = [dict(m, x2i=r0.results[c]["x2o"]) for c, m in enumerate(im)]
    r1 = run_bass_kernel_spmd(_fused_ncs[1], im1, core_ids=list(range(8)))
    im2 = [dict(m, x4i=r1.results[c]["x4o"]) for c, m in enumerate(im)]
    r2 = run_bass_kernel_spmd(_fused_ncs[2], im2, core_ids=list(range(8)))
    out = np.concatenate([r["out"] for r in r2.results], axis=0)
    return out.reshape(BSZ, SEQ, D).astype(np.float32)
